# version 34
# baseline (speedup 1.0000x reference)
import numpy as np
import ml_dtypes
import contextlib
import concourse.bass as bass
import concourse.mybir as mybir
from concourse.bass_utils import run_bass_kernel_spmd

dt = mybir.dt
F32, F16, I16 = dt.float32, dt.float16, dt.int16
ALU = mybir.AluOpType
ACT = mybir.ActivationFunctionType
AX = mybir.AxisListType

D = 1024
DFF = 2816
NCORES = 8

_c = np.float64(np.pi) / 512.0
PI_HI = float(np.float32(_c))
PI_LO = float(np.float32(_c - np.float64(np.float32(_c))))
SIXTH = 1.0 / 6.0
V_TH = [(0.25, 0.5), (0.75, 0.5)]
S_TH = [1.25, 1.75, 2.5, 3.5, 5.0]
LAD_C0 = 2.5
ACT_CONSTS = [1e-5, 1e-8, -1.0, -0.5, -1.25, -1.75, -2.5, -3.5, -5.0]

ENGS = ("sync", "scalar", "vector", "gpsimd", "tensor")
NQROT = 4


class Plan:

    def __init__(self):
        self.ops = []
        self.pos = {e: 0 for e in ENGS}
        self.incs = {e: set() for e in ENGS}
        self.qcnt = {e: 0 for e in ENGS}
        self.mem = {}
        self.fence_floor = {}

    def _regs(self, items):
        out = []
        for it in items:
            if isinstance(it, tuple):
                name, lo, hi = it[0], int(it[1]), int(it[2])
            else:
                name, lo, hi = it, 0, 1 << 40
            if name == "pst":
                lo, hi = 0, 1 << 40
            out.append((name, lo, hi))
        return out

    CAP = None

    def op(self, eng, fn, reads=(), writes=(), accum=False, dma=False):
        if Plan.CAP is not None and len(self.ops) >= Plan.CAP:
            return None
        reads = self._regs(reads)
        writes = self._regs(writes)
        deps = set()
        for (name, lo, hi) in reads:
            for (l2, h2, g, isw, acc) in self.mem.get(name, ()):
                if isw and l2 < hi and lo < h2:
                    deps.add(g)
        for (name, lo, hi) in writes:
            for (l2, h2, g, isw, acc) in self.mem.get(name, ()):
                if l2 < hi and lo < h2:
                    deps.add(g)
        gidx = len(self.ops)
        waits = dict(self.fence_floor)
        rec = dict(eng=eng, fn=fn, pos=self.pos[eng], inc=False, dma=dma,
                   waits=waits, qidx=None, accum=accum)
        if dma:
            k = self.qcnt[eng]
            rec["qidx"] = k
            self.qcnt[eng] += 1
            if k >= NQROT:
                key = ("q", eng, k % NQROT)
                waits[key] = max(waits.get(key, 0), 16 * (k // NQROT))
        self.pos[eng] += 1
        self.ops.append(rec)
        for g in deps:
            prev = self.ops[g]
            if (accum and prev["eng"] == eng and not prev["dma"]
                    and prev["accum"]):
                continue
            key, val = self._cover(g)
            waits[key] = max(waits.get(key, 0), val)
        for (name, lo, hi) in reads:
            self.mem.setdefault(name, []).append((lo, hi, gidx, False, False))
        for (name, lo, hi) in writes:
            lst = self.mem.setdefault(name, [])
            lst[:] = [e for e in lst if not (lo <= e[0] and e[1] <= hi)]
            lst.append((lo, hi, gidx, True, accum))
        return gidx

    def _cover(self, gidx):
        op = self.ops[gidx]
        e = op["eng"]
        if op["dma"]:
            k = op["qidx"]
            return (("q", e, k % NQROT), 16 * (k // NQROT + 1))
        pos = op["pos"]
        self.incs[e].add(pos)
        op["inc"] = True
        return (("e", e), pos)

    def _all_final(self):
        floor = {}
        for e in ENGS:
            last = None
            for i, o in enumerate(self.ops):
                if o["eng"] == e and not o["dma"]:
                    last = i
            if last is not None:
                key, val = self._cover(last)
                floor[key] = val
            k = self.qcnt[e]
            for r in range(min(k, NQROT)):
                floor[("q", e, r)] = 16 * len(
                    [i for i in range(k) if i % NQROT == r])
        return floor

    def fence(self):
        self.fence_floor = self._all_final()

    def finalize(self):
        return self._all_final()


def emit(nc, plan, final_waits, es):
    import bisect
    incpos = {e: sorted(plan.incs[e]) for e in ENGS}

    def xval(key, val):
        if key[0] == "e":
            return bisect.bisect_right(incpos[key[1]], val)
        return val

    sems = {}
    for e in ENGS:
        sems[("e", e)] = es.enter_context(nc.semaphore(f"S_{e}"))
        for r in range(NQROT):
            sems[("q", e, r)] = es.enter_context(nc.semaphore(f"D_{e}{r}"))
    block = es.enter_context(nc.Block())
    by_eng = {e: [o for o in plan.ops if o["eng"] == e] for e in ENGS}

    def make_section(e):
        ops = by_eng[e]

        def body(eng):
            water = {}
            for o in ops:
                for key, val in sorted(o["waits"].items(), key=lambda x: str(x[0])):
                    val = xval(key, val)
                    if water.get(key, 0) < val:
                        eng.wait_ge(sems[key], val)
                        water[key] = val
                h = o["fn"](eng)
                if o["dma"]:
                    h.then_inc(sems[("q", e, o["qidx"] % NQROT)], 16)
                elif o["inc"]:
                    h.then_inc(sems[("e", e)], 1)
            if e == "sync":
                for key, val in sorted(final_waits.items(), key=lambda x: str(x[0])):
                    val = xval(key, val)
                    if water.get(key, 0) < val:
                        eng.wait_ge(sems[key], val)

        return body

    for e in ENGS:
        getattr(block, e)(make_section(e))


_CONST_CACHE = {}


def host_consts():
    if _CONST_CACHE:
        return _CONST_CACHE
    e = np.arange(1024, dtype=np.int64)
    f = np.arange(513, dtype=np.int64)
    ef = np.outer(e, f)
    mc = ((256 - ef + 512) % 1024) - 512
    ms = ((-ef + 512) % 1024) - 512
    mc[mc == -512] = 0
    ms[ms == -512] = 0
    MF = np.concatenate([mc, ms[:, 1:512]], axis=1).astype(np.int16)
    MA = np.ascontiguousarray(MF.T)
    aw = np.full(1024, 2.0 / 1024.0, dtype=np.float32)
    aw[0] = 1.0 / 1024.0
    aw[512] = 1.0 / 1024.0
    AW = np.ascontiguousarray(aw.reshape(8, 128).T)
    t = np.arange(128)
    U = (t[:, None] <= t[None, :]).astype(np.float16)
    ONESROW = np.ones((1, 128), dtype=np.float16)
    IDENT = np.eye(128, dtype=np.float16)
    _CONST_CACHE.update(MF=MF, MA=MA, AW=AW, U=U, ONESROW=ONESROW, IDENT=IDENT)
    return _CONST_CACHE


def build(nb, seq, upto="B", qlim=None):
    NT = nb * (seq // 128)
    NCH = seq // 128
    rows = nb * seq

    nc = bass.Bass("TRN2", target_bir_lowering=False)
    h_x = nc.dram_tensor("x", [rows, D], F32, kind="ExternalInput").ap()
    h_wq = nc.dram_tensor("wq", [D, D], F32, kind="ExternalInput").ap()
    h_wk = nc.dram_tensor("wk", [D, D], F32, kind="ExternalInput").ap()
    h_wv = nc.dram_tensor("wv", [D, D], F32, kind="ExternalInput").ap()
    h_wg = nc.dram_tensor("wg", [DFF, D], F32, kind="ExternalInput").ap()
    h_wu = nc.dram_tensor("wu", [DFF, D], F32, kind="ExternalInput").ap()
    h_wd = nc.dram_tensor("wd", [D, DFF], F32, kind="ExternalInput").ap()
    h_mf = nc.dram_tensor("mf", [D, D], I16, kind="ExternalInput").ap()
    h_ma = nc.dram_tensor("ma", [D, D], I16, kind="ExternalInput").ap()
    h_aw = nc.dram_tensor("aw", [128, 8], F32, kind="ExternalInput").ap()
    h_u = nc.dram_tensor("u", [128, 128], F16, kind="ExternalInput").ap()
    h_ones = nc.dram_tensor("onesrow", [1, 128], F16, kind="ExternalInput").ap()
    h_id = nc.dram_tensor("ident", [128, 128], F16, kind="ExternalInput").ap()
    h_out = nc.dram_tensor("out", [rows, D], F32, kind="ExternalOutput").ap()
    h_x2 = nc.dram_tensor("x2spill", [rows, D], F32, kind="Internal").ap()
    h_wgq = nc.dram_tensor("wgq", [DFF, D], F16, kind="Internal").ap()
    h_wuq = nc.dram_tensor("wuq", [DFF, D], F16, kind="Internal").ap()
    h_wdq = nc.dram_tensor("wdq", [D, DFF], F16, kind="Internal").ap()

    es = contextlib.ExitStack()
    SB = lambda n, s, d=F32: es.enter_context(nc.sbuf_tensor(n, s, d))

    cpool = SB("cpool", [128, len(ACT_CONSTS)])
    for i, v in enumerate(ACT_CONSTS):
        nc.const_aps.register(F32, v, cpool[:, i:i + 1])
    u16 = SB("u16", [128, 128], F16)
    ones16 = SB("ones16", [1, 128], F16)
    onescol = SB("onescol", [128, 1], F16)
    id16 = SB("id16", [128, 128], F16)
    aw_sb = SB("aw_sb", [128, 8])
    mhalf = SB("mhalf", [128, 1])
    mone = SB("mone", [128, 1])
    x2_sb = [SB(f"x2_{r}", [128, D]) for r in range(2)]

    PSA = [es.enter_context(nc.psum_tensor(f"psa{i}", [128, D], F32))
           for i in range(3)]
    PST = es.enter_context(nc.psum_tensor("pst", [128, D], F16))
    PSC = es.enter_context(nc.psum_tensor("psc", [128, 512], F32))

    p = Plan()

    def rsqrt_pp(src_ap, src_key, tmp_ap, tmp_key, out_ap, out_key, eps):
        p.op("gpsimd", lambda g: g.tensor_scalar(
            tmp_ap, src_ap, eps, None, ALU.add),
            reads=[src_key], writes=[tmp_key])
        p.op("gpsimd", lambda g: g.tensor_tensor(
            out_ap, tmp_ap, mhalf[:], ALU.pow),
            reads=[tmp_key, "mhalf"], writes=[out_key])

    scopeA = es.enter_context(contextlib.ExitStack())
    SA = lambda n, s, d=F32: scopeA.enter_context(nc.sbuf_tensor(n, s, d))
    G16 = {k: SA(f"g16_{k}", [128, 8 * D], F16) for k in "qkv"}
    A16 = SA("a16", [128, 8 * D], F16)
    x_sb = [SA(f"x_{r}", [128, D]) for r in range(2)]
    xln = [SA(f"xln_{r}", [128, D], F16) for r in range(2)]
    xlnT = [SA(f"xlnT_{r}", [128, D], F16) for r in range(2)]
    fqh = [SA(f"fqh_{r}", [128, D], F16) for r in range(2)]
    fk16 = SA("fk16", [128, D], F16)
    fq16 = SA("fq16", [128, D], F16)
    fkh = [SA(f"fkh_{r}", [128, D], F16) for r in range(2)]
    fv_s = [SA(f"fv_{r}", [128, D], F16) for r in range(2)]
    z_s = [SA(f"z_{r}", [128, D], F16) for r in range(2)]
    w_s = [SA(f"w_{r}", [128, D], F16) for r in range(2)]
    wT = [SA(f"wT_{r}", [128, D], F16) for r in range(2)]
    zt = [[SA(f"zt_{r}_{i}", [128, 511], F16) for i in range(3)] for r in range(2)]
    wt = [[SA(f"wt_{r}_{i}", [128, 511], F16) for i in range(3)] for r in range(2)]
    uu2 = [SA(f"uu_{r}", [128, D]) for r in range(2)]
    nn = {k: [SA(f"nn_{k}{r}", [128, 513], F16) for r in range(2)] for k in "qk"}
    stats = [SA(f"stats_{r}", [128, 12]) for r in range(2)]
    mv = [SA(f"mv_{r}", [128, 2]) for r in range(2)]
    vtmp = [SA(f"vtmp_{r}", [128, 1]) for r in range(2)]
    rln = [SA(f"rln_{r}", [128, 1]) for r in range(2)]
    chi = SA("chi", [1, nb * D], F16)
    clo = SA("clo", [1, nb * D], F16)

    p.op("gpsimd", lambda g: [g.memset(cpool[:, i:i + 1], v)
                              for i, v in enumerate(ACT_CONSTS)][-1],
         writes=["cpool"])
    p.op("gpsimd", lambda g: g.memset(mhalf[:], -0.5), writes=["mhalf"])
    p.op("gpsimd", lambda g: g.memset(mone[:], -1.0), writes=["mone"])
    p.op("gpsimd", lambda g: g.memset(onescol[:], 1.0), writes=["onescol"])
    p.op("scalar", lambda s: s.dma_start(u16[:], h_u), writes=["u16"], dma=True)
    p.op("scalar", lambda s: s.dma_start(ones16[:], h_ones), writes=["ones16"], dma=True)
    p.op("scalar", lambda s: s.dma_start(id16[:], h_id), writes=["id16"], dma=True)
    p.op("scalar", lambda s: s.dma_start(aw_sb[:], h_aw), writes=["aw_sb"], dma=True)

    with contextlib.ExitStack() as scopeP:
        SP = lambda n, s, d=F32: scopeP.enter_context(nc.sbuf_tensor(n, s, d))
        F16b = SP("f16b", [128, 8 * D], F16)
        W16 = SP("w16b", [128, 8 * D], F16)
        wraw = [SP("wraw_0", [128, D])] * 2
        angb = SP("angb", [128, D])
        mfs = [SP(f"mfs_{r}", [128, D], I16) for r in range(2)]

        def twiddle_chunk(h_m, ec, dst_fn, dst_key):
            r = ec % 2
            p.op("scalar", lambda s: s.dma_start(
                mfs[r][:], h_m[ec * 128:(ec + 1) * 128, :]),
                writes=[f"mfs_{r}"], dma=True)
            p.op("vector", lambda v: v.tensor_scalar(
                angb[:], mfs[r][:], PI_HI, None, ALU.mult),
                reads=[f"mfs_{r}"], writes=["angb"])
            p.op("vector", lambda v: v.scalar_tensor_tensor(
                angb[:], mfs[r][:], PI_LO, angb[:], ALU.mult, ALU.add),
                reads=[f"mfs_{r}", "angb"], writes=["angb"])
            p.op("scalar", dst_fn, reads=["angb"], writes=[dst_key])

        for ec in range(8):
            twiddle_chunk(h_mf, ec,
                          (lambda s, ec=ec: s.activation(
                              F16b[:, ec * D:(ec + 1) * D], angb[:], ACT.Sin)),
                          ("f16b", ec * D, (ec + 1) * D))

        for kname, h_w in (("q", h_wq), ("k", h_wk), ("v", h_wv)):
            for ec in range(8):
                r = ec % 2
                p.op("scalar", lambda s, ec=ec, h_w=h_w: s.dma_start(
                    wraw[0][:], h_w[ec * 128:(ec + 1) * 128, :]),
                    writes=["wraw_0"], dma=True)
                p.op("vector", lambda v, ec=ec: v.tensor_copy(
                    W16[:, ec * D:(ec + 1) * D], wraw[0][:]),
                    reads=["wraw_0"], writes=[("w16b", ec * D, (ec + 1) * D)])
            for dc in range(8):
                for ec in range(8):
                    for hh in range(2):
                        p.op("tensor", lambda t, dc=dc, ec=ec, hh=hh: t.matmul(
                            PSA[0][:, hh * 512:(hh + 1) * 512],
                            W16[:, ec * D + dc * 128: ec * D + (dc + 1) * 128],
                            F16b[:, ec * D + hh * 512: ec * D + (hh + 1) * 512],
                            start=(ec == 0), stop=(ec == 7)),
                            reads=[("w16b", ec * D + dc * 128, ec * D + (dc + 1) * 128),
                                   ("f16b", ec * D + hh * 512, ec * D + (hh + 1) * 512)],
                            writes=[("psa0", hh * 512, (hh + 1) * 512)], accum=True)
                p.op("vector", lambda v, dc=dc, kname=kname: v.tensor_copy(
                    G16[kname][:, dc * D:(dc + 1) * D], PSA[0][:]),
                    reads=["psa0"], writes=[(f"g16_{kname}", dc * D, (dc + 1) * D)])

        for rc in range(8):
            def mk_a(s, rc=rc):
                return s.activation(angb[:], angb[:], ACT.Sin)
            twiddle_chunk(h_ma, rc, mk_a, "angb")
            p.op("vector", lambda v, rc=rc: v.tensor_scalar(
                A16[:, rc * D:(rc + 1) * D], angb[:], aw_sb[:, rc:rc + 1], None,
                ALU.mult), reads=["angb", "aw_sb"],
                writes=[("a16", rc * D, (rc + 1) * D)])

    def emit_ladder(src_ap3, src_key, w_, nb16, src2_ap, sdeq_ap, sdeq_key,
                    inv6_ap, inv6_key, out_ap3, out_key, bufs, part=None,
                    expand=True):
        if expand:
            ah, b1, b2, b3, b4, ivx, sdx = bufs
        else:
            ah, b1, b2, b3, b4 = bufs
        bc = lambda a: a.unsqueeze(2).broadcast_to((128, nb16, 16))
        a3 = lambda tens: tens[:, 0:w_].rearrange("p (b k) -> p b k", k=16)
        a2 = lambda tens: tens[:, 0:w_]
        if part in (None, 1):
            if expand:
                p.op("gpsimd", lambda g: g.tensor_copy(a3(ivx), bc(inv6_ap)),
                     reads=[inv6_key], writes=[ivx.name])
                p.op("gpsimd", lambda g: g.tensor_copy(a3(sdx), bc(sdeq_ap)),
                     reads=[sdeq_key], writes=[sdx.name])
                p.op("vector", lambda v: v.tensor_tensor(
                    a2(ah), src2_ap, a2(ivx), ALU.mult),
                    reads=[src_key, ivx.name], writes=[ah.name])
            else:
                p.op("vector", lambda v: v.tensor_tensor(
                    a3(ah), src_ap3, bc(inv6_ap), ALU.mult),
                    reads=[src_key, inv6_key], writes=[ah.name])
            p.op("vector", lambda v: v.scalar_tensor_tensor(
                a2(ah), a2(ah), -1.0, a2(ah), ALU.mult, ALU.max),
                reads=[ah.name], writes=[ah.name])
        if part == 1:
            return
        if part != 3:
            p.op("vector", lambda v: v.tensor_scalar(
                a2(b3), a2(ah), 1.25, 0.5, ALU.is_gt, ALU.mult),
                reads=[ah.name], writes=[b3.name])
            p.op("vector", lambda v: v.tensor_scalar(
                a2(b4), a2(ah), 1.75, 0.5, ALU.is_gt, ALU.mult),
                reads=[ah.name], writes=[b4.name])
            p.op("vector", lambda v: v.tensor_tensor(
                a2(b3), a2(b3), a2(b4), ALU.add),
                reads=[b3.name, b4.name], writes=[b3.name])
            p.op("vector", lambda v: v.tensor_scalar(
                a2(b1), a2(ah), 0.25, 0.5, ALU.is_gt, ALU.mult),
                reads=[ah.name], writes=[b1.name])
            p.op("vector", lambda v: v.tensor_scalar(
                a2(b2), a2(ah), 0.75, 0.5, ALU.is_gt, ALU.mult),
                reads=[ah.name], writes=[b2.name])
            p.op("vector", lambda v: v.tensor_tensor(
                a2(b1), a2(b1), a2(b2), ALU.add),
                reads=[b1.name, b2.name], writes=[b1.name])
            p.op("vector", lambda v: v.tensor_scalar(
                a2(b2), a2(ah), 2.5, 1.0, ALU.is_gt, ALU.mult),
                reads=[ah.name], writes=[b2.name])
            p.op("vector", lambda v: v.tensor_scalar(
                a2(b4), a2(ah), 3.5, 1.0, ALU.is_gt, ALU.mult),
                reads=[ah.name], writes=[b4.name])
            p.op("vector", lambda v: v.tensor_tensor(
                a2(b2), a2(b2), a2(b4), ALU.add),
                reads=[b2.name, b4.name], writes=[b2.name])
            p.op("vector", lambda v: v.tensor_scalar(
                a2(b4), a2(ah), 5.0, 2.0, ALU.is_gt, ALU.mult),
                reads=[ah.name], writes=[b4.name])
            p.op("vector", lambda v: v.tensor_tensor(
                a2(b2), a2(b2), a2(b4), ALU.add),
                reads=[b2.name, b4.name], writes=[b2.name])
            p.op("vector", lambda v: v.tensor_tensor(
                a2(b1), a2(b1), a2(b2), ALU.add),
                reads=[b1.name, b2.name], writes=[b1.name])
            p.op("vector", lambda v: v.tensor_tensor(
                a2(b1), a2(b1), a2(b3), ALU.add),
                reads=[b1.name, b3.name], writes=[b1.name])
        if part == 2:
            return
        p.op("scalar", lambda s: s.activation(
            a2(b2), src2_ap, ACT.Sign),
            reads=[src_key], writes=[b2.name])
        if expand:
            p.op("vector", lambda v: v.tensor_tensor(
                a2(b2), a2(b2), a2(sdx), ALU.mult),
                reads=[b2.name, sdx.name], writes=[b2.name])
        else:
            p.op("vector", lambda v: v.tensor_tensor(
                a3(b2), a3(b2), bc(sdeq_ap), ALU.mult),
                reads=[b2.name, sdeq_key], writes=[b2.name])
        p.op("vector", lambda v: v.tensor_tensor(
            a2(out_flat) if False else out_ap3, a3(b1), a3(b2), ALU.mult),
            reads=[b1.name, b2.name], writes=[out_key])


    p.fence()
    qraw = [SA(f"qraw_{i}", [128, D]) for i in range(2)]
    qbm = SA("qbm", [128, 64])
    qinv = SA("qinv", [128, 64])
    qsd = SA("qsd", [128, 64])
    q16b = SA("q16b", [128, D], F16)
    lbq = [SA(f"lbq_{i}", [128, D], F16) for i in range(5)]

    wjobs = []
    for h_w, h_q, nrow in ((h_wg, h_wgq, 22), (h_wu, h_wuq, 22)):
        for i in range(nrow):
            wjobs.append((h_w, h_q, i, 0, D))
    for i in range(8):
        for cc_ in range(3):
            off = cc_ * D
            wjobs.append((h_wd, h_wdq, i, off, min(D, DFF - off)))

    def emit_bqjob(wi, part=None):
        h_w, h_q, i, coff, wid = wjobs[wi]
        rq = wi % 2
        nb16 = wid // 16
        qn = h_q.tensor.name
        if part in (2, 3):
            emit_ladder(
                qraw[rq][:, 0:wid].rearrange("p (b k) -> p b k", k=16),
                f"qraw_{rq}", wid, nb16, qraw[rq][:, 0:wid],
                qsd[:, 0:nb16], "qsd", qinv[:, 0:nb16], "qinv",
                q16b[:, 0:wid].rearrange("p (b k) -> p b k", k=16), "q16b", lbq,
                part=part, expand=False)
            if part == 2:
                return
            p.op("scalar", lambda s, h_q=h_q, i=i, coff=coff, wid=wid:
                 s.dma_start(h_q[i * 128:(i + 1) * 128, coff:coff + wid],
                             q16b[:, 0:wid]),
                 reads=["q16b"], writes=[(qn, i * 100000 + coff,
                                          i * 100000 + coff + wid)], dma=True)
            return
        p.op("scalar", lambda s, h_w=h_w, i=i, coff=coff, wid=wid, rq=rq:
             s.dma_start(qraw[rq][:, 0:wid],
                         h_w[i * 128:(i + 1) * 128, coff:coff + wid]),
             writes=[f"qraw_{rq}"], dma=True)
        p.op("vector", lambda v, rq=rq, wid=wid, nb16=nb16: v.tensor_reduce(
            qbm[:, 0:nb16], qraw[rq][:, 0:wid].rearrange("p (b k) -> p b k", k=16),
            AX.X, ALU.max, apply_absolute_value=True),
            reads=[f"qraw_{rq}"], writes=["qbm"])
        p.op("vector", lambda v, nb16=nb16: v.tensor_scalar(
            qsd[:, 0:nb16], qbm[:, 0:nb16], SIXTH, 1e-12, ALU.mult, ALU.max),
            reads=["qbm"], writes=["qsd"])
        p.op("vector", lambda v, nb16=nb16: v.tensor_scalar(
            qbm[:, 0:nb16], qbm[:, 0:nb16], 1e-30, None, ALU.max),
            reads=["qbm"], writes=["qbm"])
        p.op("vector", lambda v, nb16=nb16: v.reciprocal(
            qinv[:, 0:nb16], qbm[:, 0:nb16]), reads=["qbm"], writes=["qinv"])
        p.op("vector", lambda v, nb16=nb16: v.tensor_scalar(
            qinv[:, 0:nb16], qinv[:, 0:nb16], 6.0, None, ALU.mult),
            reads=["qinv"], writes=["qinv"])
        emit_ladder(
            qraw[rq][:, 0:wid].rearrange("p (b k) -> p b k", k=16), f"qraw_{rq}",
            wid, nb16, qraw[rq][:, 0:wid],
            qsd[:, 0:nb16], "qsd", qinv[:, 0:nb16], "qinv",
            q16b[:, 0:wid].rearrange("p (b k) -> p b k", k=16), "q16b", lbq,
            part=(1 if part == 1 else None), expand=False)
        if part == 1:
            return
        p.op("scalar", lambda s, h_q=h_q, i=i, coff=coff, wid=wid:
             s.dma_start(h_q[i * 128:(i + 1) * 128, coff:coff + wid],
                         q16b[:, 0:wid]),
             reads=["q16b"], writes=[(qn, i * 100000 + coff,
                                      i * 100000 + coff + wid)], dma=True)

    if upto == "prep":
        final = p.finalize()
        emit(nc, p, final, es)
        es.close()
        return nc

    def tload(t):
        b, c = t % nb, t // nb
        r = t % 2
        row0 = b * seq + c * 128
        p.op("sync", lambda s, row0=row0, r=r: s.dma_start(
            x_sb[r][:], h_x[row0:row0 + 128, :]), writes=[f"x_{r}"], dma=True)

    for t in range(min(2, NT)):
        tload(t)

    def emit_ln_v(t):
        r = t % 2
        xk, xlk, xlTk = f"x_{r}", f"xln_{r}", f"xlnT_{r}"
        p.op("vector", lambda v, r=r: v.bn_stats(stats[r][:, 0:6], x_sb[r][:, 0:512]),
             reads=[xk], writes=[(f"stats_{r}", 0, 6)])
        p.op("vector", lambda v, r=r: v.bn_stats(stats[r][:, 6:12], x_sb[r][:, 512:1024]),
             reads=[xk], writes=[(f"stats_{r}", 6, 12)])
        p.op("vector", lambda v, r=r: v.bn_aggr(mv[r][:], stats[r][:]),
             reads=[f"stats_{r}"], writes=[f"mv_{r}"])
        rsqrt_pp(mv[r][:, 1:2], f"mv_{r}", vtmp[r][:], f"vtmp_{r}",
                 rln[r][:], f"rln_{r}", 1e-5)
        p.op("vector", lambda v, r=r: v.tensor_scalar(
            xln[r][:], x_sb[r][:], mv[r][:, 0:1], rln[r][:], ALU.subtract, ALU.mult),
            reads=[xk, f"mv_{r}", f"rln_{r}"], writes=[xlk])
        p.op("gpsimd", lambda g, r=r: g.tensor_tensor(
            x_sb[r][:], x_sb[r][:], xln[r][:], ALU.add),
            reads=[xk, xlk], writes=[xk])

    def emit_ln_tr(t):
        r = t % 2
        xlk, xlTk = f"xln_{r}", f"xlnT_{r}"
        for j in range(8):
            p.op("tensor", lambda t_, j=j, r=r: t_.transpose(
                PST[:, j * 128:(j + 1) * 128], xln[r][:, j * 128:(j + 1) * 128],
                id16[:]), reads=[(xlk, j * 128, (j + 1) * 128), "id16"],
                writes=[("pst", j * 128, (j + 1) * 128)])
        p.op("vector", lambda v, r=r: v.tensor_copy(xlnT[r][:], PST[:]),
             reads=["pst"], writes=[xlTk])

    def emit_ln(t):
        emit_ln_v(t)
        emit_ln_tr(t)

    def emit_projkv(t):
        r = t % 2
        xlTk = f"xlnT_{r}"
        for kk in range(8):
            for kname, psi in (("k", 0), ("v", 1)):
                for hh in range(2):
                    p.op("tensor", lambda t_, kk=kk, kname=kname, hh=hh, r=r, psi=psi:
                         t_.matmul(PSA[psi][:, hh * 512:(hh + 1) * 512],
                                   xlnT[r][:, kk * 128:(kk + 1) * 128],
                                   G16[kname][:, kk * D + hh * 512: kk * D + (hh + 1) * 512],
                                   start=(kk == 0), stop=(kk == 7)),
                         reads=[(xlTk, kk * 128, (kk + 1) * 128),
                                (f"g16_{kname}", kk * D + hh * 512, kk * D + (hh + 1) * 512)],
                         writes=[(f"psa{psi}", hh * 512, (hh + 1) * 512)], accum=True)

    def emit_projq(t):
        r = t % 2
        xlTk = f"xlnT_{r}"
        for kk in range(8):
            for hh in range(2):
                p.op("tensor", lambda t_, kk=kk, hh=hh, r=r:
                     t_.matmul(PSA[0][:, hh * 512:(hh + 1) * 512],
                               xlnT[r][:, kk * 128:(kk + 1) * 128],
                               G16["q"][:, kk * D + hh * 512: kk * D + (hh + 1) * 512],
                               start=(kk == 0), stop=(kk == 7)),
                     reads=[(xlTk, kk * 128, (kk + 1) * 128),
                            ("g16_q", kk * D + hh * 512, kk * D + (hh + 1) * 512)],
                     writes=[("psa0", hh * 512, (hh + 1) * 512)], accum=True)

    def emit_norm(t, kname, psi, dst):
        r = t % 2
        uk = f"uu_{r}"
        dk = f"{'fqh' if kname == 'q' else 'fkh'}_{r}"
        uux = uu2[r]
        p.op("scalar", lambda s, psi=psi, uux=uux: s.activation(
            uux[:, 0:513], PSA[psi][:, 0:513], ACT.Square),
            reads=[(f"psa{psi}", 0, 513)], writes=[(uk, 0, 513)])
        p.op("scalar", lambda s, psi=psi, uux=uux: s.activation(
            uux[:, 513:1024], PSA[psi][:, 513:1024], ACT.Square),
            reads=[(f"psa{psi}", 513, 1024)], writes=[(uk, 513, 1024)])
        p.op("vector", lambda v, uux=uux: v.tensor_tensor(
            uux[:, 1:512], uux[:, 1:512], uux[:, 513:1024], ALU.add),
            reads=[(uk, 1, 512), (uk, 513, 1024)], writes=[(uk, 1, 512)])
        p.op("scalar", lambda s, uux=uux: s.activation(
            uux[:, 0:513], uux[:, 0:513], ACT.Ln),
            reads=[(uk, 0, 513)], writes=[(uk, 0, 513)])
        p.op("scalar", lambda s, uux=uux: s.activation(
            uux[:, 0:513], uux[:, 0:513], ACT.Exp, scale=0.5),
            reads=[(uk, 0, 513)], writes=[(uk, 0, 513)])
        p.op("scalar", lambda s, uux=uux: s.activation(
            uux[:, 0:513], uux[:, 0:513], ACT.Ln, bias=1e-8, scale=1.0),
            reads=[(uk, 0, 513), "cpool"], writes=[(uk, 0, 513)])
        p.op("scalar", lambda s, kname=kname, r=r, uux=uux: s.activation(
            nn[kname][r][:], uux[:, 0:513], ACT.Exp, scale=-1.0),
            reads=[(uk, 0, 513), "cpool"], writes=[f"nn_{kname}{r}"])
        p.op("vector", lambda v, psi=psi, kname=kname, dst=dst, r=r: v.tensor_tensor(
            dst[r][:, 0:513], PSA[psi][:, 0:513], nn[kname][r][:], ALU.mult),
            reads=[(f"psa{psi}", 0, 513), f"nn_{kname}{r}"], writes=[(dk, 0, 513)])
        p.op("vector", lambda v, psi=psi, kname=kname, dst=dst, r=r: v.tensor_tensor(
            dst[r][:, 513:1024], PSA[psi][:, 513:1024], nn[kname][r][:, 1:512],
            ALU.mult),
            reads=[(f"psa{psi}", 513, 1024), f"nn_{kname}{r}"],
            writes=[(dk, 513, 1024)])

    def emit_norm2(t, kname, srct, srck, dst):
        r = t % 2
        uk = f"uu_{r}"
        dk = f"{'fqh' if kname == 'q' else 'fkh'}_{r}"
        uux = uu2[r]
        p.op("scalar", lambda s, uux=uux: s.activation(
            uux[:, 0:513], srct[:, 0:513], ACT.Square),
            reads=[(srck, 0, 513)], writes=[(uk, 0, 513)])
        p.op("scalar", lambda s, uux=uux: s.activation(
            uux[:, 513:1024], srct[:, 513:1024], ACT.Square),
            reads=[(srck, 513, 1024)], writes=[(uk, 513, 1024)])
        p.op("vector", lambda v, uux=uux: v.tensor_tensor(
            uux[:, 1:512], uux[:, 1:512], uux[:, 513:1024], ALU.add),
            reads=[(uk, 1, 512), (uk, 513, 1024)], writes=[(uk, 1, 512)])
        p.op("scalar", lambda s, uux=uux: s.activation(
            uux[:, 0:513], uux[:, 0:513], ACT.Ln),
            reads=[(uk, 0, 513)], writes=[(uk, 0, 513)])
        p.op("scalar", lambda s, uux=uux: s.activation(
            uux[:, 0:513], uux[:, 0:513], ACT.Exp, scale=0.5),
            reads=[(uk, 0, 513)], writes=[(uk, 0, 513)])
        p.op("scalar", lambda s, uux=uux: s.activation(
            uux[:, 0:513], uux[:, 0:513], ACT.Ln, bias=1e-8, scale=1.0),
            reads=[(uk, 0, 513), "cpool"], writes=[(uk, 0, 513)])
        p.op("scalar", lambda s, kname=kname, r=r, uux=uux: s.activation(
            nn[kname][r][:], uux[:, 0:513], ACT.Exp, scale=-1.0),
            reads=[(uk, 0, 513), "cpool"], writes=[f"nn_{kname}{r}"])
        p.op("vector", lambda v, kname=kname, dst=dst, srct=srct, r=r: v.tensor_tensor(
            dst[r][:, 0:513], srct[:, 0:513], nn[kname][r][:], ALU.mult),
            reads=[(srck, 0, 513), f"nn_{kname}{r}"], writes=[(dk, 0, 513)])
        p.op("vector", lambda v, kname=kname, dst=dst, srct=srct, r=r: v.tensor_tensor(
            dst[r][:, 513:1024], srct[:, 513:1024], nn[kname][r][:, 1:512],
            ALU.mult),
            reads=[(srck, 513, 1024), f"nn_{kname}{r}"],
            writes=[(dk, 513, 1024)])

    def evac_fkfv(tt):
        rr = tt % 2
        p.op("scalar", lambda s: s.copy(fk16[:], PSA[0][:]),
             reads=["psa0"], writes=["fk16"])
        p.op("scalar", lambda s, rr=rr: s.copy(fv_s[rr][:], PSA[1][:]),
             reads=["psa1"], writes=[f"fv_{rr}"])

    emit_ln(0)
    emit_projkv(0)
    evac_fkfv(0)
    for t in range(NT):
        b, c = t % nb, t // nb
        r = t % 2
        row0 = b * seq + c * 128
        xk = f"x_{r}"
        if t + 1 < NT:
            emit_ln_v(t + 1)
        emit_projq(t)
        emit_norm2(t, "k", fk16, "fk16", fkh)
        p.op("scalar", lambda s: s.copy(fq16[:], PSA[0][:]),
             reads=["psa0"], writes=["fq16"])
        if t + 1 < NT:
            emit_ln_tr(t + 1)
        emit_norm2(t, "q", fq16, "fq16", fqh)
        if t + 1 < NT:
            emit_projkv(t + 1)
        fk_, fv_, zk = f"fkh_{r}", f"fv_{r}", f"z_{r}"
        ztr = zt[r]
        p.op("vector", lambda v, r=r: v.tensor_tensor(
            z_s[r][:, 0:513], fkh[r][:, 0:513], fv_s[r][:, 0:513], ALU.mult),
            reads=[(fk_, 0, 513), (fv_, 0, 513)], writes=[(zk, 0, 513)])
        p.op("vector", lambda v, r=r, ztr=ztr: v.tensor_tensor(
            ztr[0][:], fkh[r][:, 513:1024], fv_s[r][:, 513:1024], ALU.mult),
            reads=[(fk_, 513, 1024), (fv_, 513, 1024)], writes=[f"zt_{r}_0"])
        p.op("vector", lambda v, r=r, ztr=ztr: v.tensor_tensor(
            z_s[r][:, 1:512], z_s[r][:, 1:512], ztr[0][:], ALU.subtract),
            reads=[(zk, 1, 512), f"zt_{r}_0"], writes=[(zk, 1, 512)])
        p.op("vector", lambda v, r=r, ztr=ztr: v.tensor_tensor(
            ztr[1][:], fkh[r][:, 1:512], fv_s[r][:, 513:1024], ALU.mult),
            reads=[(fk_, 1, 512), (fv_, 513, 1024)], writes=[f"zt_{r}_1"])
        p.op("vector", lambda v, r=r, ztr=ztr: v.tensor_tensor(
            ztr[2][:], fkh[r][:, 513:1024], fv_s[r][:, 1:512], ALU.mult),
            reads=[(fk_, 513, 1024), (fv_, 1, 512)], writes=[f"zt_{r}_2"])
        p.op("vector", lambda v, r=r, ztr=ztr: v.tensor_tensor(
            z_s[r][:, 513:1024], ztr[1][:], ztr[2][:], ALU.add),
            reads=[f"zt_{r}_1", f"zt_{r}_2"], writes=[(zk, 513, 1024)])
        if t < len(wjobs):
            emit_bqjob(t, part=1)
        bs = b * D
        for hh in range(2):
            p.op("tensor", lambda t_, hh=hh, r=r, c=c: t_.matmul(
                PSA[2][:, hh * 512:(hh + 1) * 512], u16[:],
                z_s[r][:, hh * 512:(hh + 1) * 512],
                start=True, stop=(c == 0)),
                reads=["u16", (zk, hh * 512, (hh + 1) * 512)],
                writes=[("psa2", hh * 512, (hh + 1) * 512)], accum=True)
            if c > 0:
                p.op("tensor", lambda t_, hh=hh, bs=bs: t_.matmul(
                    PSA[2][:, hh * 512:(hh + 1) * 512], ones16[:],
                    chi[0:1, bs + hh * 512: bs + (hh + 1) * 512],
                    start=False, stop=False),
                    reads=["ones16", ("chi", bs + hh * 512, bs + (hh + 1) * 512)],
                    writes=[("psa2", hh * 512, (hh + 1) * 512)], accum=True)
                p.op("tensor", lambda t_, hh=hh, bs=bs: t_.matmul(
                    PSA[2][:, hh * 512:(hh + 1) * 512], ones16[:],
                    clo[0:1, bs + hh * 512: bs + (hh + 1) * 512],
                    start=False, stop=True),
                    reads=["ones16", ("clo", bs + hh * 512, bs + (hh + 1) * 512)],
                    writes=[("psa2", hh * 512, (hh + 1) * 512)], accum=True)
        if c < NCH - 1:
            for hh in range(2):
                p.op("tensor", lambda t_, hh=hh, r=r, c=c: t_.matmul(
                    PSC[0:1, 0:512], onescol[:],
                    z_s[r][:, hh * 512:(hh + 1) * 512],
                    start=True, stop=(c == 0)),
                    reads=["onescol", (zk, hh * 512, (hh + 1) * 512)],
                    writes=["psc"], accum=True)
                if c > 0:
                    p.op("tensor", lambda t_, hh=hh, bs=bs: t_.matmul(
                        PSC[0:1, 0:512], ones16[0:1, 0:1],
                        chi[0:1, bs + hh * 512: bs + (hh + 1) * 512],
                        start=False, stop=False),
                        reads=["ones16", ("chi", bs + hh * 512, bs + (hh + 1) * 512)],
                        writes=["psc"], accum=True)
                    p.op("tensor", lambda t_, hh=hh, bs=bs: t_.matmul(
                        PSC[0:1, 0:512], ones16[0:1, 0:1],
                        clo[0:1, bs + hh * 512: bs + (hh + 1) * 512],
                        start=False, stop=True),
                        reads=["ones16", ("clo", bs + hh * 512, bs + (hh + 1) * 512)],
                        writes=["psc"], accum=True)
                p.op("scalar", lambda s, bs=bs, hh=hh: s.copy(
                    chi[0:1, bs + hh * 512: bs + (hh + 1) * 512], PSC[0:1, 0:512]),
                    reads=["psc"], writes=[("chi", bs + hh * 512, bs + (hh + 1) * 512)])
                p.op("vector", lambda v, bs=bs, hh=hh: v.scalar_tensor_tensor(
                    clo[0:1, bs + hh * 512: bs + (hh + 1) * 512],
                    chi[0:1, bs + hh * 512: bs + (hh + 1) * 512], -1.0,
                    PSC[0:1, 0:512], ALU.mult, ALU.add),
                    reads=[("chi", bs + hh * 512, bs + (hh + 1) * 512), "psc"],
                    writes=[("clo", bs + hh * 512, bs + (hh + 1) * 512)])
        fq_, wk_ = f"fqh_{r}", f"w_{r}"
        wtr = wt[r]
        p.op("vector", lambda v, r=r: v.tensor_tensor(
            w_s[r][:, 0:513], PSA[2][:, 0:513], fqh[r][:, 0:513], ALU.mult),
            reads=[("psa2", 0, 513), (fq_, 0, 513)], writes=[(wk_, 0, 513)])
        p.op("vector", lambda v, r=r, wtr=wtr: v.tensor_tensor(
            wtr[0][:], PSA[2][:, 513:1024], fqh[r][:, 513:1024], ALU.mult),
            reads=[("psa2", 513, 1024), (fq_, 513, 1024)], writes=[f"wt_{r}_0"])
        p.op("vector", lambda v, r=r, wtr=wtr: v.tensor_tensor(
            w_s[r][:, 1:512], w_s[r][:, 1:512], wtr[0][:], ALU.add),
            reads=[(wk_, 1, 512), f"wt_{r}_0"], writes=[(wk_, 1, 512)])
        p.op("vector", lambda v, r=r, wtr=wtr: v.tensor_tensor(
            wtr[1][:], PSA[2][:, 513:1024], fqh[r][:, 1:512], ALU.mult),
            reads=[("psa2", 513, 1024), (fq_, 1, 512)], writes=[f"wt_{r}_1"])
        p.op("vector", lambda v, r=r, wtr=wtr: v.tensor_tensor(
            wtr[2][:], PSA[2][:, 1:512], fqh[r][:, 513:1024], ALU.mult),
            reads=[("psa2", 1, 512), (fq_, 513, 1024)], writes=[f"wt_{r}_2"])
        p.op("vector", lambda v, r=r, wtr=wtr: v.tensor_tensor(
            w_s[r][:, 513:1024], wtr[1][:], wtr[2][:], ALU.subtract),
            reads=[f"wt_{r}_1", f"wt_{r}_2"], writes=[(wk_, 513, 1024)])
        for j in range(8):
            p.op("tensor", lambda t_, j=j, r=r: t_.transpose(
                PST[:, j * 128:(j + 1) * 128], w_s[r][:, j * 128:(j + 1) * 128],
                id16[:]), reads=[(wk_, j * 128, (j + 1) * 128), "id16"],
                writes=[("pst", j * 128, (j + 1) * 128)])
        p.op("scalar", lambda s, r=r: s.copy(wT[r][:], PST[:]),
             reads=["pst"], writes=[f"wT_{r}"])
        for kk in range(8):
            for hh in range(2):
                p.op("tensor", lambda t_, kk=kk, hh=hh, r=r: t_.matmul(
                    PSA[2][:, hh * 512:(hh + 1) * 512],
                    wT[r][:, kk * 128:(kk + 1) * 128],
                    A16[:, kk * D + hh * 512: kk * D + (hh + 1) * 512],
                    start=(kk == 0), stop=(kk == 7)),
                    reads=[(f"wT_{r}", kk * 128, (kk + 1) * 128),
                           ("a16", kk * D + hh * 512, kk * D + (hh + 1) * 512)],
                    writes=[("psa2", hh * 512, (hh + 1) * 512)], accum=True)
        if t < len(wjobs):
            emit_bqjob(t, part=2)
        p.op("vector", lambda v, r=r: v.tensor_tensor(
            x2_sb[r][:], x_sb[r][:], PSA[2][:], ALU.add),
            reads=[xk, "psa2"], writes=[f"x2_{r}"])
        p.op("gpsimd", lambda g, row0=row0, r=r: g.dma_start(
            h_x2[row0:row0 + 128, :], x2_sb[r][:]),
            reads=[f"x2_{r}"], writes=[("x2spill", row0, row0 + 128)], dma=True)
        if t + 2 < NT:
            tload(t + 2)
        if t + 1 < NT:
            evac_fkfv(t + 1)
        if t < len(wjobs):
            emit_bqjob(t, part=3)

    for wi in range(NT, len(wjobs)):
        emit_bqjob(wi, part=1)
        emit_bqjob(wi, part=2)
        emit_bqjob(wi, part=3)

    scopeA.close()
    p.fence()
    if upto == "A":
        final = p.finalize()
        emit(nc, p, final, es)
        es.close()
        return nc

    scopeB = es.enter_context(contextlib.ExitStack())
    SBW = lambda n, s, d=F32: scopeB.enter_context(nc.sbuf_tensor(n, s, d))
    WgT = SBW("wgT", [128, 8 * DFF], F16)
    WuT = SBW("wuT", [128, 8 * DFF], F16)
    WdT = SBW("wdT", [128, 22 * D], F16)
    for kk in range(8):
        p.op("scalar", lambda s, kk=kk: s.dma_start(
            WgT[:, kk * DFF:(kk + 1) * DFF],
            h_wgq[0:DFF, kk * 128:(kk + 1) * 128], transpose=True),
            reads=["wgq", "xpchain"], writes=[("wgT", kk * DFF, (kk + 1) * DFF),
                                              "xpchain"], dma=True)
        p.op("scalar", lambda s, kk=kk: s.dma_start(
            WuT[:, kk * DFF:(kk + 1) * DFF],
            h_wuq[0:DFF, kk * 128:(kk + 1) * 128], transpose=True),
            reads=["wuq", "xpchain"], writes=[("wuT", kk * DFF, (kk + 1) * DFF),
                                              "xpchain"], dma=True)
    for gk in range(22):
        p.op("scalar", lambda s, gk=gk: s.dma_start(
            WdT[:, gk * D:(gk + 1) * D],
            h_wdq[0:D, gk * 128:(gk + 1) * 128], transpose=True),
            reads=["wdq", "xpchain"], writes=[("wdT", gk * D, (gk + 1) * D),
                                              "xpchain"], dma=True)

    p.fence()
    if upto == "Q":
        final = p.finalize()
        emit(nc, p, final, es)
        es.close()
        return nc

    SB2 = lambda n, s, d=F32: scopeB.enter_context(nc.sbuf_tensor(n, s, d))
    xq = [SB2(f"xq_{r}", [128, D], F16) for r in range(2)]
    xqT = [SB2(f"xqT_{r}", [128, D], F16) for r in range(2)]
    hqT = [SB2(f"hqT_{r}", [128, DFF], F16) for r in range(2)]
    d2 = [SB2(f"d2_{r}", [128, D]) for r in range(2)]
    stats2 = [SB2(f"st2_{r}", [128, 12]) for r in range(2)]
    mv2 = [SB2(f"mv2_{r}", [128, 2]) for r in range(2)]
    vtmp2 = [SB2(f"vt2_{r}", [128, 1]) for r in range(2)]
    rln2 = [SB2(f"rln2_{r}", [128, 1]) for r in range(2)]
    bm2 = [SB2(f"bm2_{r}", [128, 64]) for r in range(2)]
    inv2 = [SB2(f"inv2_{r}", [128, 64]) for r in range(2)]
    sdq2 = [SB2(f"sdq2_{r}", [128, 64]) for r in range(2)]
    exb = [SB2(f"exb_{i}", [128, D], F16) for i in range(2)]
    silb = [SB2(f"sil_{r}", [128, 512], F16) for r in range(2)]
    upb = [SB2(f"up_{r}", [128, 512], F16) for r in range(2)]
    hbP = [SB2(f"hbP_{m}", [128, 1024], F16) for m in range(3)]
    hqP = [SB2(f"hqP_{m}", [128, 1024], F16) for m in range(2)]
    bmh = SB2("bmh", [128, 64])
    invh = SB2("invh", [128, 64])
    sdh = SB2("sdh", [128, 64])
    lbh = [SB2(f"lbh_{i}", [128, 1024], F16) for i in range(5)] + exb
    lbx = lbh

    JCH = []
    off = 0
    while off < DFF:
        JCH.append((off, min(512, DFF - off)))
        off += 512
    PAIRS = [(0, 1024), (1024, 1024), (2048, 768)]

    def t2load(t):
        b, c = t % nb, t // nb
        r = t % 2
        row0 = b * seq + c * 128
        p.op("sync", lambda s, row0=row0, r=r: s.dma_start(
            x2_sb[r][:], h_x2[row0:row0 + 128, :]),
            reads=[("x2spill", row0, row0 + 128)], writes=[f"x2_{r}"], dma=True)

    for t in range(min(2, NT)):
        t2load(t)

    def emit_xprefix(t, seg="abc"):
        b, c = t % nb, t // nb
        r = t % 2
        x2k = f"x2_{r}"
        if "a" in seg:
            emit_xprefix_a(r, x2k)
        if "b" in seg:
            emit_xprefix_b(r)
        if "c" in seg:
            emit_xprefix_c(r)

    def emit_xprefix_a(r, x2k):
        p.op("vector", lambda v, r=r: v.bn_stats(stats2[r][:, 0:6], x2_sb[r][:, 0:512]),
             reads=[x2k], writes=[(f"st2_{r}", 0, 6)])
        p.op("vector", lambda v, r=r: v.bn_stats(stats2[r][:, 6:12], x2_sb[r][:, 512:1024]),
             reads=[x2k], writes=[(f"st2_{r}", 6, 12)])
        p.op("vector", lambda v, r=r: v.bn_aggr(mv2[r][:], stats2[r][:]),
             reads=[f"st2_{r}"], writes=[f"mv2_{r}"])
        rsqrt_pp(mv2[r][:, 1:2], f"mv2_{r}", vtmp2[r][:], f"vt2_{r}",
                 rln2[r][:], f"rln2_{r}", 1e-5)
        p.op("vector", lambda v, r=r: v.tensor_scalar(
            d2[r][:], x2_sb[r][:], mv2[r][:, 0:1], None, ALU.subtract),
            reads=[x2k, f"mv2_{r}"], writes=[f"d2_{r}"])
        p.op("vector", lambda v, r=r: v.tensor_reduce(
            bm2[r][:], d2[r][:].rearrange("p (b k) -> p b k", k=16), AX.X,
            ALU.max, apply_absolute_value=True),
            reads=[f"d2_{r}"], writes=[f"bm2_{r}"])
        p.op("vector", lambda v, r=r: v.tensor_scalar(
            sdq2[r][:], bm2[r][:], rln2[r][:], SIXTH, ALU.mult, ALU.mult),
            reads=[f"bm2_{r}", f"rln2_{r}"], writes=[f"sdq2_{r}"])
        p.op("vector", lambda v, r=r: v.tensor_scalar(
            sdq2[r][:], sdq2[r][:], 1e-12, None, ALU.max),
            reads=[f"sdq2_{r}"], writes=[f"sdq2_{r}"])
        p.op("vector", lambda v, r=r: v.tensor_scalar(
            bm2[r][:], bm2[r][:], 1e-30, None, ALU.max),
            reads=[f"bm2_{r}"], writes=[f"bm2_{r}"])
        p.op("vector", lambda v, r=r: v.reciprocal(inv2[r][:], bm2[r][:]),
             reads=[f"bm2_{r}"], writes=[f"inv2_{r}"])
        p.op("vector", lambda v, r=r: v.tensor_scalar(
            inv2[r][:], inv2[r][:], 6.0, None, ALU.mult),
            reads=[f"inv2_{r}"], writes=[f"inv2_{r}"])

    def emit_xprefix_b(r):
        emit_ladder(
            d2[r][:].rearrange("p (b k) -> p b k", k=16), f"d2_{r}", D, 64,
            d2[r][:],
            sdq2[r][:], f"sdq2_{r}", inv2[r][:], f"inv2_{r}",
            xq[r][:].rearrange("p (b k) -> p b k", k=16), f"xq_{r}", lbx)

    def emit_xprefix_c(r):
        for j in range(8):
            p.op("tensor", lambda t_, j=j, r=r: t_.matmul(
                PSA[0][:, j * 128:(j + 1) * 128],
                xq[r][:, j * 128:(j + 1) * 128],
                id16[:], start=True, stop=True),
                reads=[(f"xq_{r}", j * 128, (j + 1) * 128), "id16"],
                writes=[("psa0", j * 128, (j + 1) * 128)])
        p.op("scalar", lambda s, r=r: s.copy(xqT[r][:], PSA[0][:]),
             reads=["psa0"], writes=[f"xqT_{r}"])


    emit_xprefix(0)
    pend_tail = None
    for t in range(NT):
        b, c = t % nb, t // nb
        r = t % 2
        rn = (t + 1) % 2
        row0 = b * seq + c * 128
        x2k = f"x2_{r}"
        def emit_gu(j, rr=None):
            joff, wj = JCH[j]
            pj = j % 2
            rr = r if rr is None else rr
            for kk in range(8):
                p.op("tensor", lambda t_, kk=kk, joff=joff, wj=wj, pj=pj, rr=rr:
                     t_.matmul(PSA[pj][:, 0:wj],
                               xqT[rr][:, kk * 128:(kk + 1) * 128],
                               WgT[:, kk * DFF + joff: kk * DFF + joff + wj],
                               start=(kk == 0), stop=(kk == 7)),
                     reads=[(f"xqT_{rr}", kk * 128, (kk + 1) * 128),
                            ("wgT", kk * DFF + joff, kk * DFF + joff + wj)],
                     writes=[(f"psa{pj}", 0, wj)], accum=True)
                p.op("tensor", lambda t_, kk=kk, joff=joff, wj=wj, pj=pj, rr=rr:
                     t_.matmul(PSA[pj][:, 512:512 + wj],
                               xqT[rr][:, kk * 128:(kk + 1) * 128],
                               WuT[:, kk * DFF + joff: kk * DFF + joff + wj],
                               start=(kk == 0), stop=(kk == 7)),
                     reads=[(f"xqT_{rr}", kk * 128, (kk + 1) * 128),
                            ("wuT", kk * DFF + joff, kk * DFF + joff + wj)],
                     writes=[(f"psa{pj}", 512, 512 + wj)], accum=True)

        def emit_h(j):
            joff, wj = JCH[j]
            pj = j % 2
            m2 = (j // 2) % 3
            hoff = (j % 2) * 512
            p.op("scalar", lambda s, pj=pj, wj=wj: s.activation(
                silb[pj][:, 0:wj], PSA[pj][:, 0:wj], ACT.Silu),
                reads=[(f"psa{pj}", 0, wj)], writes=[f"sil_{pj}"])
            p.op("scalar", lambda s, pj=pj, wj=wj: s.copy(
                upb[pj][:, 0:wj], PSA[pj][:, 512:512 + wj]),
                reads=[(f"psa{pj}", 512, 512 + wj)], writes=[f"up_{pj}"])
            p.op("vector", lambda v, pj=pj, wj=wj, m2=m2, hoff=hoff: v.tensor_tensor(
                hbP[m2][:, hoff:hoff + wj], silb[pj][:, 0:wj], upb[pj][:, 0:wj],
                ALU.mult), reads=[f"sil_{pj}", f"up_{pj}"],
                writes=[(f"hbP_{m2}", hoff, hoff + wj)])

        def emit_mid_pair(m):
            poff, pw = PAIRS[m]
            m2 = m % 3
            mq = m % 2
            nbh = pw // 16
            p.op("vector", lambda v, m2=m2, pw=pw, nbh=nbh: v.tensor_reduce(
                bmh[:, 0:nbh], hbP[m2][:, 0:pw].rearrange("p (b k) -> p b k", k=16),
                AX.X, ALU.max, apply_absolute_value=True),
                reads=[f"hbP_{m2}"], writes=["bmh"])
            p.op("vector", lambda v, nbh=nbh: v.tensor_scalar(
                sdh[:, 0:nbh], bmh[:, 0:nbh], SIXTH, 1e-12, ALU.mult, ALU.max),
                reads=["bmh"], writes=["sdh"])
            p.op("vector", lambda v, nbh=nbh: v.tensor_scalar(
                bmh[:, 0:nbh], bmh[:, 0:nbh], 1e-30, None, ALU.max),
                reads=["bmh"], writes=["bmh"])
            p.op("vector", lambda v, nbh=nbh: v.reciprocal(
                invh[:, 0:nbh], bmh[:, 0:nbh]), reads=["bmh"], writes=["invh"])
            p.op("vector", lambda v, nbh=nbh: v.tensor_scalar(
                invh[:, 0:nbh], invh[:, 0:nbh], 6.0, None, ALU.mult),
                reads=["invh"], writes=["invh"])
            emit_ladder(
                hbP[m2][:, 0:pw].rearrange("p (b k) -> p b k", k=16), f"hbP_{m2}",
                pw, nbh, hbP[m2][:, 0:pw],
                sdh[:, 0:nbh], "sdh", invh[:, 0:nbh], "invh",
                hqP[mq][:, 0:pw].rearrange("p (b k) -> p b k", k=16), f"hqP_{mq}",
                lbh)

        def emit_trdown_pair(m):
            poff, pw = PAIRS[m]
            m2 = m % 2
            for kk in range(pw // 128):
                p.op("tensor", lambda t_, kk=kk, m2=m2: t_.transpose(
                    PST[:, kk * 128:(kk + 1) * 128],
                    hqP[m2][:, kk * 128:(kk + 1) * 128],
                    id16[:]), reads=[(f"hqP_{m2}", kk * 128, (kk + 1) * 128), "id16"],
                    writes=[("pst", kk * 128, (kk + 1) * 128)])
            p.op("scalar", lambda s, poff=poff, pw=pw, r=r: s.copy(
                hqT[r][:, poff:poff + pw], PST[:, 0:pw]),
                reads=[("pst", 0, pw)], writes=[(f"hqT_{r}", poff, poff + pw)])
            for kk in range(pw // 128):
                gk = poff // 128 + kk
                for hh in range(2):
                    p.op("tensor", lambda t_, gk=gk, hh=hh, r=r:
                         t_.matmul(PSA[2][:, hh * 512:(hh + 1) * 512],
                                   hqT[r][:, gk * 128:(gk + 1) * 128],
                                   WdT[:, gk * D + hh * 512: gk * D + (hh + 1) * 512],
                                   start=(gk == 0), stop=(gk == 21)),
                         reads=[(f"hqT_{r}", gk * 128, (gk + 1) * 128),
                                ("wdT", gk * D + hh * 512, gk * D + (hh + 1) * 512)],
                         writes=[(f"psa2", hh * 512, (hh + 1) * 512)], accum=True)

        if t == 0:
            emit_gu(0)
            emit_gu(1)
            emit_h(0)
            emit_h(1)
        emit_mid_pair(0)
        if pend_tail is not None:
            pend_tail()
            pend_tail = None
        for m in range(3):
            if 2 * m + 2 < 6:
                emit_gu(2 * m + 2)
                emit_h(2 * m + 2)
                emit_gu(2 * m + 3)
                emit_h(2 * m + 3)
            if m == 2 and t + 1 < NT:
                emit_gu(0, rn)
                emit_h(0)
                emit_gu(1, rn)
                emit_h(1)
            emit_trdown_pair(m)
            if 2 * m + 2 < 6:
                emit_mid_pair(m + 1)
            if t + 1 < NT:
                if m == 0:
                    emit_xprefix_a(rn, f"x2_{rn}")
                    emit_xprefix_b(rn)
                elif m == 1:
                    emit_xprefix_c(rn)

        def tail(r=r, row0=row0, t=t, x2k=x2k):
            p.op("vector", lambda v, r=r: v.tensor_tensor(
                x2_sb[r][:], x2_sb[r][:], PSA[2][:], ALU.add),
                reads=[x2k, "psa2"], writes=[x2k])
            p.op("gpsimd", lambda g, row0=row0, r=r: g.dma_start(
                h_out[row0:row0 + 128, :], x2_sb[r][:]),
                reads=[f"x2_{r}"], writes=[("outhbm", row0, row0 + 128)], dma=True)
            if t + 2 < NT:
                t2load(t + 2)
        if t + 1 < NT:
            pend_tail = tail
        else:
            tail()

    final = p.finalize()
    emit(nc, p, final, es)
    es.close()
    return nc


def kernel(L0_attn_x, L0_attn_Wq, L0_attn_Wk, L0_attn_Wv,
           L0_ffn_w_gate, L0_ffn_w_up, L0_ffn_w_down):
    x = np.ascontiguousarray(np.asarray(L0_attn_x, dtype=np.float32))
    B, S, _ = x.shape
    nb = B // NCORES
    cc = host_consts()
    nc = build(nb, S)
    com = {
        "wq": np.ascontiguousarray(np.asarray(L0_attn_Wq, dtype=np.float32)),
        "wk": np.ascontiguousarray(np.asarray(L0_attn_Wk, dtype=np.float32)),
        "wv": np.ascontiguousarray(np.asarray(L0_attn_Wv, dtype=np.float32)),
        "wg": np.ascontiguousarray(np.asarray(L0_ffn_w_gate, dtype=np.float32)),
        "wu": np.ascontiguousarray(np.asarray(L0_ffn_w_up, dtype=np.float32)),
        "wd": np.ascontiguousarray(np.asarray(L0_ffn_w_down, dtype=np.float32)),
        "mf": cc["MF"], "ma": cc["MA"], "aw": cc["AW"], "u": cc["U"],
        "onesrow": cc["ONESROW"], "ident": cc["IDENT"],
    }
    in_maps = []
    for c in range(NCORES):
        m = dict(com)
        m["x"] = np.ascontiguousarray(x[c * nb:(c + 1) * nb].reshape(nb * S, D))
        in_maps.append(m)
    res = run_bass_kernel_spmd(nc, in_maps, core_ids=list(range(NCORES)))
    out = np.concatenate(
        [np.asarray(r["out"], dtype=np.float32).reshape(nb, S, D)
         for r in res.results], axis=0)
    return out


# version 35
# speedup vs baseline: 1.1560x; 1.1560x over previous
import numpy as np
import ml_dtypes
import contextlib
import concourse.bass as bass
import concourse.mybir as mybir
from concourse.bass_utils import run_bass_kernel_spmd

dt = mybir.dt
F32, F16, I16 = dt.float32, dt.float16, dt.int16
ALU = mybir.AluOpType
ACT = mybir.ActivationFunctionType
AX = mybir.AxisListType

D = 1024
DFF = 2816
NCORES = 8

_c = np.float64(np.pi) / 512.0
PI_HI = float(np.float32(_c))
PI_LO = float(np.float32(_c - np.float64(np.float32(_c))))
SIXTH = 1.0 / 6.0
V_TH = [(0.25, 0.5), (0.75, 0.5)]
S_TH = [1.25, 1.75, 2.5, 3.5, 5.0]
LAD_C0 = 2.5
ACT_CONSTS = [1e-5, 1e-8, -1.0, -0.5, -1.25, -1.75, -2.5, -3.5, -5.0]

ENGS = ("sync", "scalar", "vector", "gpsimd", "tensor")
NQROT = 4


class Plan:

    def __init__(self):
        self.ops = []
        self.pos = {e: 0 for e in ENGS}
        self.incs = {e: set() for e in ENGS}
        self.qcnt = {e: 0 for e in ENGS}
        self.mem = {}
        self.fence_floor = {}

    def _regs(self, items):
        out = []
        for it in items:
            if isinstance(it, tuple):
                name, lo, hi = it[0], int(it[1]), int(it[2])
            else:
                name, lo, hi = it, 0, 1 << 40
            if name == "pst":
                lo, hi = 0, 1 << 40
            out.append((name, lo, hi))
        return out

    CAP = None

    def op(self, eng, fn, reads=(), writes=(), accum=False, dma=False):
        if Plan.CAP is not None and len(self.ops) >= Plan.CAP:
            return None
        reads = self._regs(reads)
        writes = self._regs(writes)
        deps = set()
        for (name, lo, hi) in reads:
            for (l2, h2, g, isw, acc) in self.mem.get(name, ()):
                if isw and l2 < hi and lo < h2:
                    deps.add(g)
        for (name, lo, hi) in writes:
            for (l2, h2, g, isw, acc) in self.mem.get(name, ()):
                if l2 < hi and lo < h2:
                    deps.add(g)
        gidx = len(self.ops)
        waits = dict(self.fence_floor)
        rec = dict(eng=eng, fn=fn, pos=self.pos[eng], inc=False, dma=dma,
                   waits=waits, qidx=None, accum=accum)
        if dma:
            k = self.qcnt[eng]
            rec["qidx"] = k
            self.qcnt[eng] += 1
            if k >= NQROT:
                key = ("q", eng, k % NQROT)
                waits[key] = max(waits.get(key, 0), 16 * (k // NQROT))
        self.pos[eng] += 1
        self.ops.append(rec)
        for g in deps:
            prev = self.ops[g]
            if (accum and prev["eng"] == eng and not prev["dma"]
                    and prev["accum"]):
                continue
            key, val = self._cover(g)
            waits[key] = max(waits.get(key, 0), val)
        for (name, lo, hi) in reads:
            self.mem.setdefault(name, []).append((lo, hi, gidx, False, False))
        for (name, lo, hi) in writes:
            lst = self.mem.setdefault(name, [])
            lst[:] = [e for e in lst if not (lo <= e[0] and e[1] <= hi)]
            lst.append((lo, hi, gidx, True, accum))
        return gidx

    def _cover(self, gidx):
        op = self.ops[gidx]
        e = op["eng"]
        if op["dma"]:
            k = op["qidx"]
            return (("q", e, k % NQROT), 16 * (k // NQROT + 1))
        pos = op["pos"]
        self.incs[e].add(pos)
        op["inc"] = True
        return (("e", e), pos)

    def _all_final(self):
        floor = {}
        for e in ENGS:
            last = None
            for i, o in enumerate(self.ops):
                if o["eng"] == e and not o["dma"]:
                    last = i
            if last is not None:
                key, val = self._cover(last)
                floor[key] = val
            k = self.qcnt[e]
            for r in range(min(k, NQROT)):
                floor[("q", e, r)] = 16 * len(
                    [i for i in range(k) if i % NQROT == r])
        return floor

    def fence(self):
        self.fence_floor = self._all_final()

    def finalize(self):
        return self._all_final()


def emit(nc, plan, final_waits, es):
    import bisect
    incpos = {e: sorted(plan.incs[e]) for e in ENGS}

    def xval(key, val):
        if key[0] == "e":
            return bisect.bisect_right(incpos[key[1]], val)
        return val

    sems = {}
    for e in ENGS:
        sems[("e", e)] = es.enter_context(nc.semaphore(f"S_{e}"))
        for r in range(NQROT):
            sems[("q", e, r)] = es.enter_context(nc.semaphore(f"D_{e}{r}"))
    block = es.enter_context(nc.Block())
    by_eng = {e: [o for o in plan.ops if o["eng"] == e] for e in ENGS}

    def make_section(e):
        ops = by_eng[e]

        def body(eng):
            water = {}
            for o in ops:
                for key, val in sorted(o["waits"].items(), key=lambda x: str(x[0])):
                    val = xval(key, val)
                    if water.get(key, 0) < val:
                        eng.wait_ge(sems[key], val)
                        water[key] = val
                h = o["fn"](eng)
                if o["dma"]:
                    h.then_inc(sems[("q", e, o["qidx"] % NQROT)], 16)
                elif o["inc"]:
                    h.then_inc(sems[("e", e)], 1)
            if e == "sync":
                for key, val in sorted(final_waits.items(), key=lambda x: str(x[0])):
                    val = xval(key, val)
                    if water.get(key, 0) < val:
                        eng.wait_ge(sems[key], val)

        return body

    for e in ENGS:
        getattr(block, e)(make_section(e))


_CONST_CACHE = {}


def host_consts():
    if _CONST_CACHE:
        return _CONST_CACHE
    e = np.arange(1024, dtype=np.int64)
    f = np.arange(513, dtype=np.int64)
    ef = np.outer(e, f)
    mc = ((256 - ef + 512) % 1024) - 512
    ms = ((-ef + 512) % 1024) - 512
    mc[mc == -512] = 0
    ms[ms == -512] = 0
    MF = np.concatenate([mc, ms[:, 1:512]], axis=1).astype(np.int16)
    MA = np.ascontiguousarray(MF.T)
    aw = np.full(1024, 2.0 / 1024.0, dtype=np.float32)
    aw[0] = 1.0 / 1024.0
    aw[512] = 1.0 / 1024.0
    AW = np.ascontiguousarray(aw.reshape(8, 128).T)
    t = np.arange(128)
    U = (t[:, None] <= t[None, :]).astype(np.float16)
    ONESROW = np.ones((1, 128), dtype=np.float16)
    IDENT = np.eye(128, dtype=np.float16)
    _CONST_CACHE.update(MF=MF, MA=MA, AW=AW, U=U, ONESROW=ONESROW, IDENT=IDENT)
    return _CONST_CACHE


def build(nb, seq, upto="B", qlim=None):
    NT = nb * (seq // 128)
    NCH = seq // 128
    rows = nb * seq

    nc = bass.Bass("TRN2", target_bir_lowering=False)
    h_x = nc.dram_tensor("x", [rows, D], F32, kind="ExternalInput").ap()
    h_wq = nc.dram_tensor("wq", [D, D], F32, kind="ExternalInput").ap()
    h_wk = nc.dram_tensor("wk", [D, D], F32, kind="ExternalInput").ap()
    h_wv = nc.dram_tensor("wv", [D, D], F32, kind="ExternalInput").ap()
    h_wg = nc.dram_tensor("wg", [DFF, D], F32, kind="ExternalInput").ap()
    h_wu = nc.dram_tensor("wu", [DFF, D], F32, kind="ExternalInput").ap()
    h_wd = nc.dram_tensor("wd", [D, DFF], F32, kind="ExternalInput").ap()
    h_mf = nc.dram_tensor("mf", [D, D], I16, kind="ExternalInput").ap()
    h_ma = nc.dram_tensor("ma", [D, D], I16, kind="ExternalInput").ap()
    h_aw = nc.dram_tensor("aw", [128, 8], F32, kind="ExternalInput").ap()
    h_u = nc.dram_tensor("u", [128, 128], F16, kind="ExternalInput").ap()
    h_ones = nc.dram_tensor("onesrow", [1, 128], F16, kind="ExternalInput").ap()
    h_id = nc.dram_tensor("ident", [128, 128], F16, kind="ExternalInput").ap()
    h_out = nc.dram_tensor("out", [rows, D], F32, kind="ExternalOutput").ap()
    h_x2 = nc.dram_tensor("x2spill", [rows, D], F32, kind="Internal").ap()
    h_wgq = nc.dram_tensor("wgq", [DFF, D], F16, kind="Internal").ap()
    h_wuq = nc.dram_tensor("wuq", [DFF, D], F16, kind="Internal").ap()
    h_wdq = nc.dram_tensor("wdq", [D, DFF], F16, kind="Internal").ap()

    es = contextlib.ExitStack()
    SB = lambda n, s, d=F32: es.enter_context(nc.sbuf_tensor(n, s, d))

    cpool = SB("cpool", [128, len(ACT_CONSTS)])
    for i, v in enumerate(ACT_CONSTS):
        nc.const_aps.register(F32, v, cpool[:, i:i + 1])
    u16 = SB("u16", [128, 128], F16)
    ones16 = SB("ones16", [1, 128], F16)
    onescol = SB("onescol", [128, 1], F16)
    id16 = SB("id16", [128, 128], F16)
    aw_sb = SB("aw_sb", [128, 8])
    mhalf = SB("mhalf", [128, 1])
    mone = SB("mone", [128, 1])
    x2_sb = [SB(f"x2_{r}", [128, D]) for r in range(2)]

    PSA = [es.enter_context(nc.psum_tensor(f"psa{i}", [128, D], F32))
           for i in range(3)]
    PST = es.enter_context(nc.psum_tensor("pst", [128, D], F16))
    PSC = es.enter_context(nc.psum_tensor("psc", [128, 512], F32))

    p = Plan()

    def rsqrt_pp(src_ap, src_key, tmp_ap, tmp_key, out_ap, out_key, eps):
        p.op("gpsimd", lambda g: g.tensor_scalar(
            tmp_ap, src_ap, eps, None, ALU.add),
            reads=[src_key], writes=[tmp_key])
        p.op("gpsimd", lambda g: g.tensor_tensor(
            out_ap, tmp_ap, mhalf[:], ALU.pow),
            reads=[tmp_key, "mhalf"], writes=[out_key])

    scopeA = es.enter_context(contextlib.ExitStack())
    SA = lambda n, s, d=F32: scopeA.enter_context(nc.sbuf_tensor(n, s, d))
    G16 = {k: SA(f"g16_{k}", [128, 8 * D], F16) for k in "qkv"}
    A16 = SA("a16", [128, 8 * D], F16)
    x_sb = [SA(f"x_{r}", [128, D]) for r in range(2)]
    xln = [SA(f"xln_{r}", [128, D], F16) for r in range(2)]
    xlnT = [SA(f"xlnT_{r}", [128, D], F16) for r in range(2)]
    fqh = [SA(f"fqh_{r}", [128, D], F16) for r in range(2)]
    fk16 = SA("fk16", [128, D], F16)
    fq16 = SA("fq16", [128, D], F16)
    fkh = [SA(f"fkh_{r}", [128, D], F16) for r in range(2)]
    fv_s = [SA(f"fv_{r}", [128, D], F16) for r in range(2)]
    z_s = [SA(f"z_{r}", [128, D], F16) for r in range(2)]
    w_s = [SA(f"w_{r}", [128, D], F16) for r in range(2)]
    wT = [SA(f"wT_{r}", [128, D], F16) for r in range(2)]
    zt = [[SA(f"zt_{r}_{i}", [128, 511], F16) for i in range(3)] for r in range(2)]
    wt = [[SA(f"wt_{r}_{i}", [128, 511], F16) for i in range(3)] for r in range(2)]
    uu2 = [SA(f"uu_{r}", [128, D]) for r in range(2)]
    nn = {k: [SA(f"nn_{k}{r}", [128, 513], F16) for r in range(2)] for k in "qk"}
    stats = [SA(f"stats_{r}", [128, 12]) for r in range(2)]
    mv = [SA(f"mv_{r}", [128, 2]) for r in range(2)]
    vtmp = [SA(f"vtmp_{r}", [128, 1]) for r in range(2)]
    rln = [SA(f"rln_{r}", [128, 1]) for r in range(2)]
    chi = SA("chi", [1, nb * D], F16)
    clo = SA("clo", [1, nb * D], F16)

    p.op("gpsimd", lambda g: [g.memset(cpool[:, i:i + 1], v)
                              for i, v in enumerate(ACT_CONSTS)][-1],
         writes=["cpool"])
    p.op("gpsimd", lambda g: g.memset(mhalf[:], -0.5), writes=["mhalf"])
    p.op("gpsimd", lambda g: g.memset(mone[:], -1.0), writes=["mone"])
    p.op("gpsimd", lambda g: g.memset(onescol[:], 1.0), writes=["onescol"])
    p.op("scalar", lambda s: s.dma_start(u16[:], h_u), writes=["u16"], dma=True)
    p.op("scalar", lambda s: s.dma_start(ones16[:], h_ones), writes=["ones16"], dma=True)
    p.op("scalar", lambda s: s.dma_start(id16[:], h_id), writes=["id16"], dma=True)
    p.op("scalar", lambda s: s.dma_start(aw_sb[:], h_aw), writes=["aw_sb"], dma=True)

    with contextlib.ExitStack() as scopeP:
        SP = lambda n, s, d=F32: scopeP.enter_context(nc.sbuf_tensor(n, s, d))
        F16b = SP("f16b", [128, 8 * D], F16)
        W16 = SP("w16b", [128, 8 * D], F16)
        wraw = [SP("wraw_0", [128, D])] * 2
        angb = SP("angb", [128, D])
        mfs = [SP(f"mfs_{r}", [128, D], I16) for r in range(2)]

        def twiddle_chunk(h_m, ec, dst_fn, dst_key):
            r = ec % 2
            p.op("scalar", lambda s: s.dma_start(
                mfs[r][:], h_m[ec * 128:(ec + 1) * 128, :]),
                writes=[f"mfs_{r}"], dma=True)
            p.op("vector", lambda v: v.tensor_scalar(
                angb[:], mfs[r][:], PI_HI, None, ALU.mult),
                reads=[f"mfs_{r}"], writes=["angb"])
            p.op("vector", lambda v: v.scalar_tensor_tensor(
                angb[:], mfs[r][:], PI_LO, angb[:], ALU.mult, ALU.add),
                reads=[f"mfs_{r}", "angb"], writes=["angb"])
            p.op("scalar", dst_fn, reads=["angb"], writes=[dst_key])

        for ec in range(8):
            twiddle_chunk(h_mf, ec,
                          (lambda s, ec=ec: s.activation(
                              F16b[:, ec * D:(ec + 1) * D], angb[:], ACT.Sin)),
                          ("f16b", ec * D, (ec + 1) * D))

        for kname, h_w in (("q", h_wq), ("k", h_wk), ("v", h_wv)):
            for ec in range(8):
                r = ec % 2
                p.op("scalar", lambda s, ec=ec, h_w=h_w: s.dma_start(
                    wraw[0][:], h_w[ec * 128:(ec + 1) * 128, :]),
                    writes=["wraw_0"], dma=True)
                p.op("vector", lambda v, ec=ec: v.tensor_copy(
                    W16[:, ec * D:(ec + 1) * D], wraw[0][:]),
                    reads=["wraw_0"], writes=[("w16b", ec * D, (ec + 1) * D)])
            for dc in range(8):
                for ec in range(8):
                    for hh in range(2):
                        p.op("tensor", lambda t, dc=dc, ec=ec, hh=hh: t.matmul(
                            PSA[0][:, hh * 512:(hh + 1) * 512],
                            W16[:, ec * D + dc * 128: ec * D + (dc + 1) * 128],
                            F16b[:, ec * D + hh * 512: ec * D + (hh + 1) * 512],
                            start=(ec == 0), stop=(ec == 7)),
                            reads=[("w16b", ec * D + dc * 128, ec * D + (dc + 1) * 128),
                                   ("f16b", ec * D + hh * 512, ec * D + (hh + 1) * 512)],
                            writes=[("psa0", hh * 512, (hh + 1) * 512)], accum=True)
                p.op("vector", lambda v, dc=dc, kname=kname: v.tensor_copy(
                    G16[kname][:, dc * D:(dc + 1) * D], PSA[0][:]),
                    reads=["psa0"], writes=[(f"g16_{kname}", dc * D, (dc + 1) * D)])

        for rc in range(8):
            def mk_a(s, rc=rc):
                return s.activation(angb[:], angb[:], ACT.Sin)
            twiddle_chunk(h_ma, rc, mk_a, "angb")
            p.op("vector", lambda v, rc=rc: v.tensor_scalar(
                A16[:, rc * D:(rc + 1) * D], angb[:], aw_sb[:, rc:rc + 1], None,
                ALU.mult), reads=["angb", "aw_sb"],
                writes=[("a16", rc * D, (rc + 1) * D)])

    def emit_ladder(src_ap3, src_key, w_, nb16, src2_ap, sdeq_ap, sdeq_key,
                    inv6_ap, inv6_key, out_ap3, out_key, bufs, part=None,
                    abs_on_v=False):
        ah, b1, b2, b3, b4 = bufs
        bc = lambda a: a.unsqueeze(2).broadcast_to((128, nb16, 16))
        a3 = lambda tens: tens[:, 0:w_].rearrange("p (b k) -> p b k", k=16)
        a2 = lambda tens: tens[:, 0:w_]
        if part in (None, 1):
            p.op("vector", lambda v: v.tensor_tensor(
                a3(ah), src_ap3, bc(inv6_ap), ALU.mult),
                reads=[src_key, inv6_key], writes=[ah.name])
            if abs_on_v:
                p.op("vector", lambda v: v.scalar_tensor_tensor(
                    a2(ah), a2(ah), -1.0, a2(ah), ALU.mult, ALU.max),
                    reads=[ah.name], writes=[ah.name])
            else:
                p.op("scalar", lambda s: s.activation(
                    a2(ah), a2(ah), ACT.Abs),
                    reads=[ah.name], writes=[ah.name])
        if part == 1:
            return
        if part != 3:
            p.op("vector", lambda v: v.tensor_scalar(
                a2(b3), a2(ah), 1.25, 0.5, ALU.is_gt, ALU.mult),
                reads=[ah.name], writes=[b3.name])
            p.op("vector", lambda v: v.tensor_scalar(
                a2(b4), a2(ah), 1.75, 0.5, ALU.is_gt, ALU.mult),
                reads=[ah.name], writes=[b4.name])
            p.op("vector", lambda v: v.tensor_tensor(
                a2(b3), a2(b3), a2(b4), ALU.add),
                reads=[b3.name, b4.name], writes=[b3.name])
            p.op("vector", lambda v: v.tensor_scalar(
                a2(b1), a2(ah), 0.25, 0.5, ALU.is_gt, ALU.mult),
                reads=[ah.name], writes=[b1.name])
            p.op("vector", lambda v: v.tensor_scalar(
                a2(b2), a2(ah), 0.75, 0.5, ALU.is_gt, ALU.mult),
                reads=[ah.name], writes=[b2.name])
            p.op("vector", lambda v: v.tensor_tensor(
                a2(b1), a2(b1), a2(b2), ALU.add),
                reads=[b1.name, b2.name], writes=[b1.name])
            p.op("vector", lambda v: v.tensor_scalar(
                a2(b2), a2(ah), 2.5, 1.0, ALU.is_gt, ALU.mult),
                reads=[ah.name], writes=[b2.name])
            p.op("vector", lambda v: v.tensor_scalar(
                a2(b4), a2(ah), 3.5, 1.0, ALU.is_gt, ALU.mult),
                reads=[ah.name], writes=[b4.name])
            p.op("vector", lambda v: v.tensor_tensor(
                a2(b2), a2(b2), a2(b4), ALU.add),
                reads=[b2.name, b4.name], writes=[b2.name])
            p.op("vector", lambda v: v.tensor_scalar(
                a2(b4), a2(ah), 5.0, 2.0, ALU.is_gt, ALU.mult),
                reads=[ah.name], writes=[b4.name])
            p.op("vector", lambda v: v.tensor_tensor(
                a2(b2), a2(b2), a2(b4), ALU.add),
                reads=[b2.name, b4.name], writes=[b2.name])
            p.op("vector", lambda v: v.tensor_tensor(
                a2(b1), a2(b1), a2(b2), ALU.add),
                reads=[b1.name, b2.name], writes=[b1.name])
            p.op("vector", lambda v: v.tensor_tensor(
                a2(b1), a2(b1), a2(b3), ALU.add),
                reads=[b1.name, b3.name], writes=[b1.name])
        if part == 2:
            return
        p.op("scalar", lambda s: s.activation(
            a2(b2), src2_ap, ACT.Sign),
            reads=[src_key], writes=[b2.name])
        p.op("vector", lambda v: v.tensor_tensor(
            a3(b2), a3(b2), bc(sdeq_ap), ALU.mult),
            reads=[b2.name, sdeq_key], writes=[b2.name])
        p.op("vector", lambda v: v.tensor_tensor(
            a2(out_flat) if False else out_ap3, a3(b1), a3(b2), ALU.mult),
            reads=[b1.name, b2.name], writes=[out_key])


    p.fence()
    qraw = [SA(f"qraw_{i}", [128, D]) for i in range(2)]
    qbm = SA("qbm", [128, 64])
    qinv = SA("qinv", [128, 64])
    qsd = SA("qsd", [128, 64])
    q16b = SA("q16b", [128, D], F16)
    lbq = [SA(f"lbq_{i}", [128, D], F16) for i in range(5)]

    wjobs = []
    for h_w, h_q, nrow in ((h_wg, h_wgq, 22), (h_wu, h_wuq, 22)):
        for i in range(nrow):
            wjobs.append((h_w, h_q, i, 0, D))
    for i in range(8):
        for cc_ in range(3):
            off = cc_ * D
            wjobs.append((h_wd, h_wdq, i, off, min(D, DFF - off)))

    def emit_bqjob(wi, part=None):
        h_w, h_q, i, coff, wid = wjobs[wi]
        rq = wi % 2
        nb16 = wid // 16
        qn = h_q.tensor.name
        if part in (2, 3):
            emit_ladder(
                qraw[rq][:, 0:wid].rearrange("p (b k) -> p b k", k=16),
                f"qraw_{rq}", wid, nb16, qraw[rq][:, 0:wid],
                qsd[:, 0:nb16], "qsd", qinv[:, 0:nb16], "qinv",
                q16b[:, 0:wid].rearrange("p (b k) -> p b k", k=16), "q16b", lbq,
                part=part)
            if part == 2:
                return
            p.op("scalar", lambda s, h_q=h_q, i=i, coff=coff, wid=wid:
                 s.dma_start(h_q[i * 128:(i + 1) * 128, coff:coff + wid],
                             q16b[:, 0:wid]),
                 reads=["q16b"], writes=[(qn, i * 100000 + coff,
                                          i * 100000 + coff + wid)], dma=True)
            return
        p.op("scalar", lambda s, h_w=h_w, i=i, coff=coff, wid=wid, rq=rq:
             s.dma_start(qraw[rq][:, 0:wid],
                         h_w[i * 128:(i + 1) * 128, coff:coff + wid]),
             writes=[f"qraw_{rq}"], dma=True)
        p.op("vector", lambda v, rq=rq, wid=wid, nb16=nb16: v.tensor_reduce(
            qbm[:, 0:nb16], qraw[rq][:, 0:wid].rearrange("p (b k) -> p b k", k=16),
            AX.X, ALU.max, apply_absolute_value=True),
            reads=[f"qraw_{rq}"], writes=["qbm"])
        p.op("vector", lambda v, nb16=nb16: v.tensor_scalar(
            qsd[:, 0:nb16], qbm[:, 0:nb16], SIXTH, 1e-12, ALU.mult, ALU.max),
            reads=["qbm"], writes=["qsd"])
        p.op("vector", lambda v, nb16=nb16: v.tensor_scalar(
            qbm[:, 0:nb16], qbm[:, 0:nb16], 1e-30, None, ALU.max),
            reads=["qbm"], writes=["qbm"])
        p.op("vector", lambda v, nb16=nb16: v.reciprocal(
            qinv[:, 0:nb16], qbm[:, 0:nb16]), reads=["qbm"], writes=["qinv"])
        p.op("vector", lambda v, nb16=nb16: v.tensor_scalar(
            qinv[:, 0:nb16], qinv[:, 0:nb16], 6.0, None, ALU.mult),
            reads=["qinv"], writes=["qinv"])
        emit_ladder(
            qraw[rq][:, 0:wid].rearrange("p (b k) -> p b k", k=16), f"qraw_{rq}",
            wid, nb16, qraw[rq][:, 0:wid],
            qsd[:, 0:nb16], "qsd", qinv[:, 0:nb16], "qinv",
            q16b[:, 0:wid].rearrange("p (b k) -> p b k", k=16), "q16b", lbq,
            part=(1 if part == 1 else None))
        if part == 1:
            return
        p.op("scalar", lambda s, h_q=h_q, i=i, coff=coff, wid=wid:
             s.dma_start(h_q[i * 128:(i + 1) * 128, coff:coff + wid],
                         q16b[:, 0:wid]),
             reads=["q16b"], writes=[(qn, i * 100000 + coff,
                                      i * 100000 + coff + wid)], dma=True)

    if upto == "prep":
        final = p.finalize()
        emit(nc, p, final, es)
        es.close()
        return nc

    def tload(t):
        b, c = t % nb, t // nb
        r = t % 2
        row0 = b * seq + c * 128
        p.op("sync", lambda s, row0=row0, r=r: s.dma_start(
            x_sb[r][:], h_x[row0:row0 + 128, :]), writes=[f"x_{r}"], dma=True)

    for t in range(min(2, NT)):
        tload(t)

    def emit_ln_v(t):
        r = t % 2
        xk, xlk, xlTk = f"x_{r}", f"xln_{r}", f"xlnT_{r}"
        p.op("vector", lambda v, r=r: v.bn_stats(stats[r][:, 0:6], x_sb[r][:, 0:512]),
             reads=[xk], writes=[(f"stats_{r}", 0, 6)])
        p.op("vector", lambda v, r=r: v.bn_stats(stats[r][:, 6:12], x_sb[r][:, 512:1024]),
             reads=[xk], writes=[(f"stats_{r}", 6, 12)])
        p.op("vector", lambda v, r=r: v.bn_aggr(mv[r][:], stats[r][:]),
             reads=[f"stats_{r}"], writes=[f"mv_{r}"])
        rsqrt_pp(mv[r][:, 1:2], f"mv_{r}", vtmp[r][:], f"vtmp_{r}",
                 rln[r][:], f"rln_{r}", 1e-5)
        p.op("vector", lambda v, r=r: v.tensor_scalar(
            xln[r][:], x_sb[r][:], mv[r][:, 0:1], rln[r][:], ALU.subtract, ALU.mult),
            reads=[xk, f"mv_{r}", f"rln_{r}"], writes=[xlk])
        p.op("gpsimd", lambda g, r=r: g.tensor_tensor(
            x_sb[r][:], x_sb[r][:], xln[r][:], ALU.add),
            reads=[xk, xlk], writes=[xk])

    def emit_ln_tr(t):
        r = t % 2
        xlk, xlTk = f"xln_{r}", f"xlnT_{r}"
        for j in range(8):
            p.op("tensor", lambda t_, j=j, r=r: t_.transpose(
                PST[:, j * 128:(j + 1) * 128], xln[r][:, j * 128:(j + 1) * 128],
                id16[:]), reads=[(xlk, j * 128, (j + 1) * 128), "id16"],
                writes=[("pst", j * 128, (j + 1) * 128)])
        p.op("vector", lambda v, r=r: v.tensor_copy(xlnT[r][:], PST[:]),
             reads=["pst"], writes=[xlTk])

    def emit_ln(t):
        emit_ln_v(t)
        emit_ln_tr(t)

    def emit_projkv(t):
        r = t % 2
        xlTk = f"xlnT_{r}"
        for kk in range(8):
            for kname, psi in (("k", 0), ("v", 1)):
                for hh in range(2):
                    p.op("tensor", lambda t_, kk=kk, kname=kname, hh=hh, r=r, psi=psi:
                         t_.matmul(PSA[psi][:, hh * 512:(hh + 1) * 512],
                                   xlnT[r][:, kk * 128:(kk + 1) * 128],
                                   G16[kname][:, kk * D + hh * 512: kk * D + (hh + 1) * 512],
                                   start=(kk == 0), stop=(kk == 7)),
                         reads=[(xlTk, kk * 128, (kk + 1) * 128),
                                (f"g16_{kname}", kk * D + hh * 512, kk * D + (hh + 1) * 512)],
                         writes=[(f"psa{psi}", hh * 512, (hh + 1) * 512)], accum=True)

    def emit_projq(t):
        r = t % 2
        xlTk = f"xlnT_{r}"
        for kk in range(8):
            for hh in range(2):
                p.op("tensor", lambda t_, kk=kk, hh=hh, r=r:
                     t_.matmul(PSA[0][:, hh * 512:(hh + 1) * 512],
                               xlnT[r][:, kk * 128:(kk + 1) * 128],
                               G16["q"][:, kk * D + hh * 512: kk * D + (hh + 1) * 512],
                               start=(kk == 0), stop=(kk == 7)),
                     reads=[(xlTk, kk * 128, (kk + 1) * 128),
                            ("g16_q", kk * D + hh * 512, kk * D + (hh + 1) * 512)],
                     writes=[("psa0", hh * 512, (hh + 1) * 512)], accum=True)

    def emit_norm(t, kname, psi, dst):
        r = t % 2
        uk = f"uu_{r}"
        dk = f"{'fqh' if kname == 'q' else 'fkh'}_{r}"
        uux = uu2[r]
        p.op("scalar", lambda s, psi=psi, uux=uux: s.activation(
            uux[:, 0:513], PSA[psi][:, 0:513], ACT.Square),
            reads=[(f"psa{psi}", 0, 513)], writes=[(uk, 0, 513)])
        p.op("scalar", lambda s, psi=psi, uux=uux: s.activation(
            uux[:, 513:1024], PSA[psi][:, 513:1024], ACT.Square),
            reads=[(f"psa{psi}", 513, 1024)], writes=[(uk, 513, 1024)])
        p.op("vector", lambda v, uux=uux: v.tensor_tensor(
            uux[:, 1:512], uux[:, 1:512], uux[:, 513:1024], ALU.add),
            reads=[(uk, 1, 512), (uk, 513, 1024)], writes=[(uk, 1, 512)])
        p.op("scalar", lambda s, uux=uux: s.activation(
            uux[:, 0:513], uux[:, 0:513], ACT.Ln),
            reads=[(uk, 0, 513)], writes=[(uk, 0, 513)])
        p.op("scalar", lambda s, uux=uux: s.activation(
            uux[:, 0:513], uux[:, 0:513], ACT.Exp, scale=0.5),
            reads=[(uk, 0, 513)], writes=[(uk, 0, 513)])
        p.op("scalar", lambda s, uux=uux: s.activation(
            uux[:, 0:513], uux[:, 0:513], ACT.Ln, bias=1e-8, scale=1.0),
            reads=[(uk, 0, 513), "cpool"], writes=[(uk, 0, 513)])
        p.op("scalar", lambda s, kname=kname, r=r, uux=uux: s.activation(
            nn[kname][r][:], uux[:, 0:513], ACT.Exp, scale=-1.0),
            reads=[(uk, 0, 513), "cpool"], writes=[f"nn_{kname}{r}"])
        p.op("vector", lambda v, psi=psi, kname=kname, dst=dst, r=r: v.tensor_tensor(
            dst[r][:, 0:513], PSA[psi][:, 0:513], nn[kname][r][:], ALU.mult),
            reads=[(f"psa{psi}", 0, 513), f"nn_{kname}{r}"], writes=[(dk, 0, 513)])
        p.op("vector", lambda v, psi=psi, kname=kname, dst=dst, r=r: v.tensor_tensor(
            dst[r][:, 513:1024], PSA[psi][:, 513:1024], nn[kname][r][:, 1:512],
            ALU.mult),
            reads=[(f"psa{psi}", 513, 1024), f"nn_{kname}{r}"],
            writes=[(dk, 513, 1024)])

    def emit_norm2(t, kname, srct, srck, dst):
        r = t % 2
        uk = f"uu_{r}"
        dk = f"{'fqh' if kname == 'q' else 'fkh'}_{r}"
        uux = uu2[r]
        p.op("scalar", lambda s, uux=uux: s.activation(
            uux[:, 0:513], srct[:, 0:513], ACT.Square),
            reads=[(srck, 0, 513)], writes=[(uk, 0, 513)])
        p.op("scalar", lambda s, uux=uux: s.activation(
            uux[:, 513:1024], srct[:, 513:1024], ACT.Square),
            reads=[(srck, 513, 1024)], writes=[(uk, 513, 1024)])
        p.op("vector", lambda v, uux=uux: v.tensor_tensor(
            uux[:, 1:512], uux[:, 1:512], uux[:, 513:1024], ALU.add),
            reads=[(uk, 1, 512), (uk, 513, 1024)], writes=[(uk, 1, 512)])
        p.op("scalar", lambda s, uux=uux: s.activation(
            uux[:, 0:513], uux[:, 0:513], ACT.Ln),
            reads=[(uk, 0, 513)], writes=[(uk, 0, 513)])
        p.op("scalar", lambda s, uux=uux: s.activation(
            uux[:, 0:513], uux[:, 0:513], ACT.Exp, scale=0.5),
            reads=[(uk, 0, 513)], writes=[(uk, 0, 513)])
        p.op("scalar", lambda s, uux=uux: s.activation(
            uux[:, 0:513], uux[:, 0:513], ACT.Ln, bias=1e-8, scale=1.0),
            reads=[(uk, 0, 513), "cpool"], writes=[(uk, 0, 513)])
        p.op("scalar", lambda s, kname=kname, r=r, uux=uux: s.activation(
            nn[kname][r][:], uux[:, 0:513], ACT.Exp, scale=-1.0),
            reads=[(uk, 0, 513), "cpool"], writes=[f"nn_{kname}{r}"])
        p.op("vector", lambda v, kname=kname, dst=dst, srct=srct, r=r: v.tensor_tensor(
            dst[r][:, 0:513], srct[:, 0:513], nn[kname][r][:], ALU.mult),
            reads=[(srck, 0, 513), f"nn_{kname}{r}"], writes=[(dk, 0, 513)])
        p.op("vector", lambda v, kname=kname, dst=dst, srct=srct, r=r: v.tensor_tensor(
            dst[r][:, 513:1024], srct[:, 513:1024], nn[kname][r][:, 1:512],
            ALU.mult),
            reads=[(srck, 513, 1024), f"nn_{kname}{r}"],
            writes=[(dk, 513, 1024)])

    def evac_fkfv(tt):
        rr = tt % 2
        p.op("scalar", lambda s: s.copy(fk16[:], PSA[0][:]),
             reads=["psa0"], writes=["fk16"])
        p.op("scalar", lambda s, rr=rr: s.copy(fv_s[rr][:], PSA[1][:]),
             reads=["psa1"], writes=[f"fv_{rr}"])

    emit_ln(0)
    emit_projkv(0)
    evac_fkfv(0)
    for t in range(NT):
        b, c = t % nb, t // nb
        r = t % 2
        row0 = b * seq + c * 128
        xk = f"x_{r}"
        if t + 1 < NT:
            emit_ln_v(t + 1)
        emit_projq(t)
        emit_norm2(t, "k", fk16, "fk16", fkh)
        p.op("scalar", lambda s: s.copy(fq16[:], PSA[0][:]),
             reads=["psa0"], writes=["fq16"])
        if t + 1 < NT:
            emit_ln_tr(t + 1)
        emit_norm2(t, "q", fq16, "fq16", fqh)
        if t + 1 < NT:
            emit_projkv(t + 1)
        fk_, fv_, zk = f"fkh_{r}", f"fv_{r}", f"z_{r}"
        ztr = zt[r]
        p.op("vector", lambda v, r=r: v.tensor_tensor(
            z_s[r][:, 0:513], fkh[r][:, 0:513], fv_s[r][:, 0:513], ALU.mult),
            reads=[(fk_, 0, 513), (fv_, 0, 513)], writes=[(zk, 0, 513)])
        p.op("vector", lambda v, r=r, ztr=ztr: v.tensor_tensor(
            ztr[0][:], fkh[r][:, 513:1024], fv_s[r][:, 513:1024], ALU.mult),
            reads=[(fk_, 513, 1024), (fv_, 513, 1024)], writes=[f"zt_{r}_0"])
        p.op("vector", lambda v, r=r, ztr=ztr: v.tensor_tensor(
            z_s[r][:, 1:512], z_s[r][:, 1:512], ztr[0][:], ALU.subtract),
            reads=[(zk, 1, 512), f"zt_{r}_0"], writes=[(zk, 1, 512)])
        p.op("vector", lambda v, r=r, ztr=ztr: v.tensor_tensor(
            ztr[1][:], fkh[r][:, 1:512], fv_s[r][:, 513:1024], ALU.mult),
            reads=[(fk_, 1, 512), (fv_, 513, 1024)], writes=[f"zt_{r}_1"])
        p.op("vector", lambda v, r=r, ztr=ztr: v.tensor_tensor(
            ztr[2][:], fkh[r][:, 513:1024], fv_s[r][:, 1:512], ALU.mult),
            reads=[(fk_, 513, 1024), (fv_, 1, 512)], writes=[f"zt_{r}_2"])
        p.op("vector", lambda v, r=r, ztr=ztr: v.tensor_tensor(
            z_s[r][:, 513:1024], ztr[1][:], ztr[2][:], ALU.add),
            reads=[f"zt_{r}_1", f"zt_{r}_2"], writes=[(zk, 513, 1024)])
        if t < len(wjobs):
            emit_bqjob(t, part=1)
        bs = b * D
        for hh in range(2):
            p.op("tensor", lambda t_, hh=hh, r=r, c=c: t_.matmul(
                PSA[2][:, hh * 512:(hh + 1) * 512], u16[:],
                z_s[r][:, hh * 512:(hh + 1) * 512],
                start=True, stop=(c == 0)),
                reads=["u16", (zk, hh * 512, (hh + 1) * 512)],
                writes=[("psa2", hh * 512, (hh + 1) * 512)], accum=True)
            if c > 0:
                p.op("tensor", lambda t_, hh=hh, bs=bs: t_.matmul(
                    PSA[2][:, hh * 512:(hh + 1) * 512], ones16[:],
                    chi[0:1, bs + hh * 512: bs + (hh + 1) * 512],
                    start=False, stop=False),
                    reads=["ones16", ("chi", bs + hh * 512, bs + (hh + 1) * 512)],
                    writes=[("psa2", hh * 512, (hh + 1) * 512)], accum=True)
                p.op("tensor", lambda t_, hh=hh, bs=bs: t_.matmul(
                    PSA[2][:, hh * 512:(hh + 1) * 512], ones16[:],
                    clo[0:1, bs + hh * 512: bs + (hh + 1) * 512],
                    start=False, stop=True),
                    reads=["ones16", ("clo", bs + hh * 512, bs + (hh + 1) * 512)],
                    writes=[("psa2", hh * 512, (hh + 1) * 512)], accum=True)
        if c < NCH - 1:
            for hh in range(2):
                p.op("tensor", lambda t_, hh=hh, r=r, c=c: t_.matmul(
                    PSC[0:1, 0:512], onescol[:],
                    z_s[r][:, hh * 512:(hh + 1) * 512],
                    start=True, stop=(c == 0)),
                    reads=["onescol", (zk, hh * 512, (hh + 1) * 512)],
                    writes=["psc"], accum=True)
                if c > 0:
                    p.op("tensor", lambda t_, hh=hh, bs=bs: t_.matmul(
                        PSC[0:1, 0:512], ones16[0:1, 0:1],
                        chi[0:1, bs + hh * 512: bs + (hh + 1) * 512],
                        start=False, stop=False),
                        reads=["ones16", ("chi", bs + hh * 512, bs + (hh + 1) * 512)],
                        writes=["psc"], accum=True)
                    p.op("tensor", lambda t_, hh=hh, bs=bs: t_.matmul(
                        PSC[0:1, 0:512], ones16[0:1, 0:1],
                        clo[0:1, bs + hh * 512: bs + (hh + 1) * 512],
                        start=False, stop=True),
                        reads=["ones16", ("clo", bs + hh * 512, bs + (hh + 1) * 512)],
                        writes=["psc"], accum=True)
                p.op("scalar", lambda s, bs=bs, hh=hh: s.copy(
                    chi[0:1, bs + hh * 512: bs + (hh + 1) * 512], PSC[0:1, 0:512]),
                    reads=["psc"], writes=[("chi", bs + hh * 512, bs + (hh + 1) * 512)])
                p.op("vector", lambda v, bs=bs, hh=hh: v.scalar_tensor_tensor(
                    clo[0:1, bs + hh * 512: bs + (hh + 1) * 512],
                    chi[0:1, bs + hh * 512: bs + (hh + 1) * 512], -1.0,
                    PSC[0:1, 0:512], ALU.mult, ALU.add),
                    reads=[("chi", bs + hh * 512, bs + (hh + 1) * 512), "psc"],
                    writes=[("clo", bs + hh * 512, bs + (hh + 1) * 512)])
        fq_, wk_ = f"fqh_{r}", f"w_{r}"
        wtr = wt[r]
        p.op("vector", lambda v, r=r: v.tensor_tensor(
            w_s[r][:, 0:513], PSA[2][:, 0:513], fqh[r][:, 0:513], ALU.mult),
            reads=[("psa2", 0, 513), (fq_, 0, 513)], writes=[(wk_, 0, 513)])
        p.op("vector", lambda v, r=r, wtr=wtr: v.tensor_tensor(
            wtr[0][:], PSA[2][:, 513:1024], fqh[r][:, 513:1024], ALU.mult),
            reads=[("psa2", 513, 1024), (fq_, 513, 1024)], writes=[f"wt_{r}_0"])
        p.op("vector", lambda v, r=r, wtr=wtr: v.tensor_tensor(
            w_s[r][:, 1:512], w_s[r][:, 1:512], wtr[0][:], ALU.add),
            reads=[(wk_, 1, 512), f"wt_{r}_0"], writes=[(wk_, 1, 512)])
        p.op("vector", lambda v, r=r, wtr=wtr: v.tensor_tensor(
            wtr[1][:], PSA[2][:, 513:1024], fqh[r][:, 1:512], ALU.mult),
            reads=[("psa2", 513, 1024), (fq_, 1, 512)], writes=[f"wt_{r}_1"])
        p.op("vector", lambda v, r=r, wtr=wtr: v.tensor_tensor(
            wtr[2][:], PSA[2][:, 1:512], fqh[r][:, 513:1024], ALU.mult),
            reads=[("psa2", 1, 512), (fq_, 513, 1024)], writes=[f"wt_{r}_2"])
        p.op("vector", lambda v, r=r, wtr=wtr: v.tensor_tensor(
            w_s[r][:, 513:1024], wtr[1][:], wtr[2][:], ALU.subtract),
            reads=[f"wt_{r}_1", f"wt_{r}_2"], writes=[(wk_, 513, 1024)])
        for j in range(8):
            p.op("tensor", lambda t_, j=j, r=r: t_.transpose(
                PST[:, j * 128:(j + 1) * 128], w_s[r][:, j * 128:(j + 1) * 128],
                id16[:]), reads=[(wk_, j * 128, (j + 1) * 128), "id16"],
                writes=[("pst", j * 128, (j + 1) * 128)])
        p.op("scalar", lambda s, r=r: s.copy(wT[r][:], PST[:]),
             reads=["pst"], writes=[f"wT_{r}"])
        for kk in range(8):
            for hh in range(2):
                p.op("tensor", lambda t_, kk=kk, hh=hh, r=r: t_.matmul(
                    PSA[2][:, hh * 512:(hh + 1) * 512],
                    wT[r][:, kk * 128:(kk + 1) * 128],
                    A16[:, kk * D + hh * 512: kk * D + (hh + 1) * 512],
                    start=(kk == 0), stop=(kk == 7)),
                    reads=[(f"wT_{r}", kk * 128, (kk + 1) * 128),
                           ("a16", kk * D + hh * 512, kk * D + (hh + 1) * 512)],
                    writes=[("psa2", hh * 512, (hh + 1) * 512)], accum=True)
        if t < len(wjobs):
            emit_bqjob(t, part=2)
        p.op("vector", lambda v, r=r: v.tensor_tensor(
            x2_sb[r][:], x_sb[r][:], PSA[2][:], ALU.add),
            reads=[xk, "psa2"], writes=[f"x2_{r}"])
        p.op("gpsimd", lambda g, row0=row0, r=r: g.dma_start(
            h_x2[row0:row0 + 128, :], x2_sb[r][:]),
            reads=[f"x2_{r}"], writes=[("x2spill", row0, row0 + 128)], dma=True)
        if t + 2 < NT:
            tload(t + 2)
        if t + 1 < NT:
            evac_fkfv(t + 1)
        if t < len(wjobs):
            emit_bqjob(t, part=3)

    for wi in range(NT, len(wjobs)):
        emit_bqjob(wi, part=1)
        emit_bqjob(wi, part=2)
        emit_bqjob(wi, part=3)

    scopeA.close()
    p.fence()
    if upto == "A":
        final = p.finalize()
        emit(nc, p, final, es)
        es.close()
        return nc

    scopeB = es.enter_context(contextlib.ExitStack())
    SBW = lambda n, s, d=F32: scopeB.enter_context(nc.sbuf_tensor(n, s, d))
    WgT = SBW("wgT", [128, 8 * DFF], F16)
    WuT = SBW("wuT", [128, 8 * DFF], F16)
    WdT = SBW("wdT", [128, 22 * D], F16)
    for kk in range(8):
        p.op("scalar", lambda s, kk=kk: s.dma_start(
            WgT[:, kk * DFF:(kk + 1) * DFF],
            h_wgq[0:DFF, kk * 128:(kk + 1) * 128], transpose=True),
            reads=["wgq", "xpchain"], writes=[("wgT", kk * DFF, (kk + 1) * DFF),
                                              "xpchain"], dma=True)
        p.op("scalar", lambda s, kk=kk: s.dma_start(
            WuT[:, kk * DFF:(kk + 1) * DFF],
            h_wuq[0:DFF, kk * 128:(kk + 1) * 128], transpose=True),
            reads=["wuq", "xpchain"], writes=[("wuT", kk * DFF, (kk + 1) * DFF),
                                              "xpchain"], dma=True)
    for gk in range(22):
        p.op("scalar", lambda s, gk=gk: s.dma_start(
            WdT[:, gk * D:(gk + 1) * D],
            h_wdq[0:D, gk * 128:(gk + 1) * 128], transpose=True),
            reads=["wdq", "xpchain"], writes=[("wdT", gk * D, (gk + 1) * D),
                                              "xpchain"], dma=True)

    p.fence()
    if upto == "Q":
        final = p.finalize()
        emit(nc, p, final, es)
        es.close()
        return nc

    SB2 = lambda n, s, d=F32: scopeB.enter_context(nc.sbuf_tensor(n, s, d))
    xq = [SB2(f"xq_{r}", [128, D], F16) for r in range(2)]
    xqT = [SB2(f"xqT_{r}", [128, D], F16) for r in range(2)]
    hqT = [SB2(f"hqT_{r}", [128, DFF], F16) for r in range(2)]
    d2 = [SB2(f"d2_{r}", [128, D]) for r in range(2)]
    stats2 = [SB2(f"st2_{r}", [128, 12]) for r in range(2)]
    mv2 = [SB2(f"mv2_{r}", [128, 2]) for r in range(2)]
    vtmp2 = [SB2(f"vt2_{r}", [128, 1]) for r in range(2)]
    rln2 = [SB2(f"rln2_{r}", [128, 1]) for r in range(2)]
    bm2 = [SB2(f"bm2_{r}", [128, 64]) for r in range(2)]
    inv2 = [SB2(f"inv2_{r}", [128, 64]) for r in range(2)]
    sdq2 = [SB2(f"sdq2_{r}", [128, 64]) for r in range(2)]
    lbx = [SB2(f"lbx_{i}", [128, D], F16) for i in range(5)]
    silb = [SB2(f"sil_{r}", [128, 512], F16) for r in range(2)]
    upb = [SB2(f"up_{r}", [128, 512], F16) for r in range(2)]
    hbP = [SB2(f"hbP_{m}", [128, 1024], F16) for m in range(2)]
    hqP = [SB2(f"hqP_{m}", [128, 1024], F16) for m in range(2)]
    bmh = SB2("bmh", [128, 64])
    invh = SB2("invh", [128, 64])
    sdh = SB2("sdh", [128, 64])
    lbh = [SB2(f"lbh_{i}", [128, 1024], F16) for i in range(5)]

    JCH = []
    off = 0
    while off < DFF:
        JCH.append((off, min(512, DFF - off)))
        off += 512
    PAIRS = [(0, 1024), (1024, 1024), (2048, 768)]

    def t2load(t):
        b, c = t % nb, t // nb
        r = t % 2
        row0 = b * seq + c * 128
        p.op("sync", lambda s, row0=row0, r=r: s.dma_start(
            x2_sb[r][:], h_x2[row0:row0 + 128, :]),
            reads=[("x2spill", row0, row0 + 128)], writes=[f"x2_{r}"], dma=True)

    for t in range(min(2, NT)):
        t2load(t)

    def emit_xprefix(t, seg="abc"):
        b, c = t % nb, t // nb
        r = t % 2
        x2k = f"x2_{r}"
        if "a" in seg:
            emit_xprefix_a(r, x2k)
        if "b" in seg:
            emit_xprefix_b(r)
        if "c" in seg:
            emit_xprefix_c(r)

    def emit_xprefix_a(r, x2k):
        p.op("vector", lambda v, r=r: v.bn_stats(stats2[r][:, 0:6], x2_sb[r][:, 0:512]),
             reads=[x2k], writes=[(f"st2_{r}", 0, 6)])
        p.op("vector", lambda v, r=r: v.bn_stats(stats2[r][:, 6:12], x2_sb[r][:, 512:1024]),
             reads=[x2k], writes=[(f"st2_{r}", 6, 12)])
        p.op("vector", lambda v, r=r: v.bn_aggr(mv2[r][:], stats2[r][:]),
             reads=[f"st2_{r}"], writes=[f"mv2_{r}"])
        rsqrt_pp(mv2[r][:, 1:2], f"mv2_{r}", vtmp2[r][:], f"vt2_{r}",
                 rln2[r][:], f"rln2_{r}", 1e-5)
        p.op("vector", lambda v, r=r: v.tensor_scalar(
            d2[r][:], x2_sb[r][:], mv2[r][:, 0:1], None, ALU.subtract),
            reads=[x2k, f"mv2_{r}"], writes=[f"d2_{r}"])
        p.op("vector", lambda v, r=r: v.tensor_reduce(
            bm2[r][:], d2[r][:].rearrange("p (b k) -> p b k", k=16), AX.X,
            ALU.max, apply_absolute_value=True),
            reads=[f"d2_{r}"], writes=[f"bm2_{r}"])
        p.op("vector", lambda v, r=r: v.tensor_scalar(
            sdq2[r][:], bm2[r][:], rln2[r][:], SIXTH, ALU.mult, ALU.mult),
            reads=[f"bm2_{r}", f"rln2_{r}"], writes=[f"sdq2_{r}"])
        p.op("vector", lambda v, r=r: v.tensor_scalar(
            sdq2[r][:], sdq2[r][:], 1e-12, None, ALU.max),
            reads=[f"sdq2_{r}"], writes=[f"sdq2_{r}"])
        p.op("vector", lambda v, r=r: v.tensor_scalar(
            bm2[r][:], bm2[r][:], 1e-30, None, ALU.max),
            reads=[f"bm2_{r}"], writes=[f"bm2_{r}"])
        p.op("vector", lambda v, r=r: v.reciprocal(inv2[r][:], bm2[r][:]),
             reads=[f"bm2_{r}"], writes=[f"inv2_{r}"])
        p.op("vector", lambda v, r=r: v.tensor_scalar(
            inv2[r][:], inv2[r][:], 6.0, None, ALU.mult),
            reads=[f"inv2_{r}"], writes=[f"inv2_{r}"])

    def emit_xprefix_b(r):
        emit_ladder(
            d2[r][:].rearrange("p (b k) -> p b k", k=16), f"d2_{r}", D, 64,
            d2[r][:],
            sdq2[r][:], f"sdq2_{r}", inv2[r][:], f"inv2_{r}",
            xq[r][:].rearrange("p (b k) -> p b k", k=16), f"xq_{r}", lbx,
            abs_on_v=True)

    def emit_xprefix_c(r):
        for j in range(8):
            p.op("tensor", lambda t_, j=j, r=r: t_.matmul(
                PSA[0][:, j * 128:(j + 1) * 128],
                xq[r][:, j * 128:(j + 1) * 128],
                id16[:], start=True, stop=True),
                reads=[(f"xq_{r}", j * 128, (j + 1) * 128), "id16"],
                writes=[("psa0", j * 128, (j + 1) * 128)])
        p.op("scalar", lambda s, r=r: s.copy(xqT[r][:], PSA[0][:]),
             reads=["psa0"], writes=[f"xqT_{r}"])


    emit_xprefix(0)
    pend_tail = None
    for t in range(NT):
        b, c = t % nb, t // nb
        r = t % 2
        rn = (t + 1) % 2
        row0 = b * seq + c * 128
        x2k = f"x2_{r}"
        def emit_gu(j, rr=None):
            joff, wj = JCH[j]
            pj = j % 2
            rr = r if rr is None else rr
            for kk in range(8):
                p.op("tensor", lambda t_, kk=kk, joff=joff, wj=wj, pj=pj, rr=rr:
                     t_.matmul(PSA[pj][:, 0:wj],
                               xqT[rr][:, kk * 128:(kk + 1) * 128],
                               WgT[:, kk * DFF + joff: kk * DFF + joff + wj],
                               start=(kk == 0), stop=(kk == 7)),
                     reads=[(f"xqT_{rr}", kk * 128, (kk + 1) * 128),
                            ("wgT", kk * DFF + joff, kk * DFF + joff + wj)],
                     writes=[(f"psa{pj}", 0, wj)], accum=True)
                p.op("tensor", lambda t_, kk=kk, joff=joff, wj=wj, pj=pj, rr=rr:
                     t_.matmul(PSA[pj][:, 512:512 + wj],
                               xqT[rr][:, kk * 128:(kk + 1) * 128],
                               WuT[:, kk * DFF + joff: kk * DFF + joff + wj],
                               start=(kk == 0), stop=(kk == 7)),
                     reads=[(f"xqT_{rr}", kk * 128, (kk + 1) * 128),
                            ("wuT", kk * DFF + joff, kk * DFF + joff + wj)],
                     writes=[(f"psa{pj}", 512, 512 + wj)], accum=True)

        def emit_h(j):
            joff, wj = JCH[j]
            pj = j % 2
            m2 = (j // 2) % 2
            hoff = (j % 2) * 512
            p.op("scalar", lambda s, pj=pj, wj=wj: s.activation(
                silb[pj][:, 0:wj], PSA[pj][:, 0:wj], ACT.Silu),
                reads=[(f"psa{pj}", 0, wj)], writes=[f"sil_{pj}"])
            p.op("scalar", lambda s, pj=pj, wj=wj: s.copy(
                upb[pj][:, 0:wj], PSA[pj][:, 512:512 + wj]),
                reads=[(f"psa{pj}", 512, 512 + wj)], writes=[f"up_{pj}"])
            p.op("vector", lambda v, pj=pj, wj=wj, m2=m2, hoff=hoff: v.tensor_tensor(
                hbP[m2][:, hoff:hoff + wj], silb[pj][:, 0:wj], upb[pj][:, 0:wj],
                ALU.mult), reads=[f"sil_{pj}", f"up_{pj}"],
                writes=[(f"hbP_{m2}", hoff, hoff + wj)])

        def emit_mid_pair(m):
            poff, pw = PAIRS[m]
            m2 = m % 2
            nbh = pw // 16
            p.op("vector", lambda v, m2=m2, pw=pw, nbh=nbh: v.tensor_reduce(
                bmh[:, 0:nbh], hbP[m2][:, 0:pw].rearrange("p (b k) -> p b k", k=16),
                AX.X, ALU.max, apply_absolute_value=True),
                reads=[f"hbP_{m2}"], writes=["bmh"])
            p.op("vector", lambda v, nbh=nbh: v.tensor_scalar(
                sdh[:, 0:nbh], bmh[:, 0:nbh], SIXTH, 1e-12, ALU.mult, ALU.max),
                reads=["bmh"], writes=["sdh"])
            p.op("vector", lambda v, nbh=nbh: v.tensor_scalar(
                bmh[:, 0:nbh], bmh[:, 0:nbh], 1e-30, None, ALU.max),
                reads=["bmh"], writes=["bmh"])
            p.op("vector", lambda v, nbh=nbh: v.reciprocal(
                invh[:, 0:nbh], bmh[:, 0:nbh]), reads=["bmh"], writes=["invh"])
            p.op("vector", lambda v, nbh=nbh: v.tensor_scalar(
                invh[:, 0:nbh], invh[:, 0:nbh], 6.0, None, ALU.mult),
                reads=["invh"], writes=["invh"])
            emit_ladder(
                hbP[m2][:, 0:pw].rearrange("p (b k) -> p b k", k=16), f"hbP_{m2}",
                pw, nbh, hbP[m2][:, 0:pw],
                sdh[:, 0:nbh], "sdh", invh[:, 0:nbh], "invh",
                hqP[m2][:, 0:pw].rearrange("p (b k) -> p b k", k=16), f"hqP_{m2}",
                lbh, abs_on_v=True)

        def emit_trdown_pair(m):
            poff, pw = PAIRS[m]
            m2 = m % 2
            for kk in range(pw // 128):
                p.op("tensor", lambda t_, kk=kk, m2=m2: t_.transpose(
                    PST[:, kk * 128:(kk + 1) * 128],
                    hqP[m2][:, kk * 128:(kk + 1) * 128],
                    id16[:]), reads=[(f"hqP_{m2}", kk * 128, (kk + 1) * 128), "id16"],
                    writes=[("pst", kk * 128, (kk + 1) * 128)])
            p.op("scalar", lambda s, poff=poff, pw=pw, r=r: s.copy(
                hqT[r][:, poff:poff + pw], PST[:, 0:pw]),
                reads=[("pst", 0, pw)], writes=[(f"hqT_{r}", poff, poff + pw)])
            for kk in range(pw // 128):
                gk = poff // 128 + kk
                for hh in range(2):
                    p.op("tensor", lambda t_, gk=gk, hh=hh, r=r:
                         t_.matmul(PSA[2][:, hh * 512:(hh + 1) * 512],
                                   hqT[r][:, gk * 128:(gk + 1) * 128],
                                   WdT[:, gk * D + hh * 512: gk * D + (hh + 1) * 512],
                                   start=(gk == 0), stop=(gk == 21)),
                         reads=[(f"hqT_{r}", gk * 128, (gk + 1) * 128),
                                ("wdT", gk * D + hh * 512, gk * D + (hh + 1) * 512)],
                         writes=[(f"psa2", hh * 512, (hh + 1) * 512)], accum=True)

        if t == 0:
            emit_gu(0)
            emit_gu(1)
            emit_h(0)
            emit_h(1)
        emit_mid_pair(0)
        if pend_tail is not None:
            pend_tail()
            pend_tail = None
        for m in range(3):
            if 2 * m + 2 < 6:
                emit_gu(2 * m + 2)
                emit_h(2 * m + 2)
                emit_gu(2 * m + 3)
                emit_h(2 * m + 3)
            if m == 2 and t + 1 < NT:
                emit_gu(0, rn)
                emit_h(0)
                emit_gu(1, rn)
                emit_h(1)
            emit_trdown_pair(m)
            if 2 * m + 2 < 6:
                emit_mid_pair(m + 1)
            if t + 1 < NT:
                if m == 0:
                    emit_xprefix_a(rn, f"x2_{rn}")
                    emit_xprefix_b(rn)
                elif m == 1:
                    emit_xprefix_c(rn)

        def tail(r=r, row0=row0, t=t, x2k=x2k):
            p.op("vector", lambda v, r=r: v.tensor_tensor(
                x2_sb[r][:], x2_sb[r][:], PSA[2][:], ALU.add),
                reads=[x2k, "psa2"], writes=[x2k])
            p.op("gpsimd", lambda g, row0=row0, r=r: g.dma_start(
                h_out[row0:row0 + 128, :], x2_sb[r][:]),
                reads=[f"x2_{r}"], writes=[("outhbm", row0, row0 + 128)], dma=True)
            if t + 2 < NT:
                t2load(t + 2)
        if t + 1 < NT:
            pend_tail = tail
        else:
            tail()

    final = p.finalize()
    emit(nc, p, final, es)
    es.close()
    return nc


def kernel(L0_attn_x, L0_attn_Wq, L0_attn_Wk, L0_attn_Wv,
           L0_ffn_w_gate, L0_ffn_w_up, L0_ffn_w_down):
    x = np.ascontiguousarray(np.asarray(L0_attn_x, dtype=np.float32))
    B, S, _ = x.shape
    nb = B // NCORES
    cc = host_consts()
    nc = build(nb, S)
    com = {
        "wq": np.ascontiguousarray(np.asarray(L0_attn_Wq, dtype=np.float32)),
        "wk": np.ascontiguousarray(np.asarray(L0_attn_Wk, dtype=np.float32)),
        "wv": np.ascontiguousarray(np.asarray(L0_attn_Wv, dtype=np.float32)),
        "wg": np.ascontiguousarray(np.asarray(L0_ffn_w_gate, dtype=np.float32)),
        "wu": np.ascontiguousarray(np.asarray(L0_ffn_w_up, dtype=np.float32)),
        "wd": np.ascontiguousarray(np.asarray(L0_ffn_w_down, dtype=np.float32)),
        "mf": cc["MF"], "ma": cc["MA"], "aw": cc["AW"], "u": cc["U"],
        "onesrow": cc["ONESROW"], "ident": cc["IDENT"],
    }
    in_maps = []
    for c in range(NCORES):
        m = dict(com)
        m["x"] = np.ascontiguousarray(x[c * nb:(c + 1) * nb].reshape(nb * S, D))
        in_maps.append(m)
    res = run_bass_kernel_spmd(nc, in_maps, core_ids=list(range(NCORES)))
    out = np.concatenate(
        [np.asarray(r["out"], dtype=np.float32).reshape(nb, S, D)
         for r in res.results], axis=0)
    return out


# version 36 (final)
# speedup vs baseline: 1.2198x; 1.0551x over previous
import numpy as np
import ml_dtypes
import contextlib
import concourse.bass as bass
import concourse.mybir as mybir
from concourse.bass_utils import run_bass_kernel_spmd

dt = mybir.dt
F32, F16, I16 = dt.float32, dt.float16, dt.int16
ALU = mybir.AluOpType
ACT = mybir.ActivationFunctionType
AX = mybir.AxisListType

D = 1024
DFF = 2816
NCORES = 8

_c = np.float64(np.pi) / 512.0
PI_HI = float(np.float32(_c))
PI_LO = float(np.float32(_c - np.float64(np.float32(_c))))
SIXTH = 1.0 / 6.0
V_TH = [(0.25, 0.5), (0.75, 0.5)]
S_TH = [1.25, 1.75, 2.5, 3.5, 5.0]
LAD_C0 = 2.5
ACT_CONSTS = [1e-5, 1e-8, -1.0, -0.5, -1.25, -1.75, -2.5, -3.5, -5.0]

ENGS = ("sync", "scalar", "vector", "gpsimd", "tensor")
NQROT = 4


class Plan:

    def __init__(self):
        self.ops = []
        self.pos = {e: 0 for e in ENGS}
        self.incs = {e: set() for e in ENGS}
        self.qcnt = {e: 0 for e in ENGS}
        self.mem = {}
        self.fence_floor = {}

    def _regs(self, items):
        out = []
        for it in items:
            if isinstance(it, tuple):
                name, lo, hi = it[0], int(it[1]), int(it[2])
            else:
                name, lo, hi = it, 0, 1 << 40
            if name == "pst":
                lo, hi = 0, 1 << 40
            out.append((name, lo, hi))
        return out

    CAP = None

    def op(self, eng, fn, reads=(), writes=(), accum=False, dma=False):
        if Plan.CAP is not None and len(self.ops) >= Plan.CAP:
            return None
        reads = self._regs(reads)
        writes = self._regs(writes)
        deps = set()
        for (name, lo, hi) in reads:
            for (l2, h2, g, isw, acc) in self.mem.get(name, ()):
                if isw and l2 < hi and lo < h2:
                    deps.add(g)
        for (name, lo, hi) in writes:
            for (l2, h2, g, isw, acc) in self.mem.get(name, ()):
                if l2 < hi and lo < h2:
                    deps.add(g)
        gidx = len(self.ops)
        waits = dict(self.fence_floor)
        rec = dict(eng=eng, fn=fn, pos=self.pos[eng], inc=False, dma=dma,
                   waits=waits, qidx=None, accum=accum)
        if dma:
            k = self.qcnt[eng]
            rec["qidx"] = k
            self.qcnt[eng] += 1
            if k >= NQROT:
                key = ("q", eng, k % NQROT)
                waits[key] = max(waits.get(key, 0), 16 * (k // NQROT))
        self.pos[eng] += 1
        self.ops.append(rec)
        for g in deps:
            prev = self.ops[g]
            if (accum and prev["eng"] == eng and not prev["dma"]
                    and prev["accum"]):
                continue
            key, val = self._cover(g)
            waits[key] = max(waits.get(key, 0), val)
        for (name, lo, hi) in reads:
            self.mem.setdefault(name, []).append((lo, hi, gidx, False, False))
        for (name, lo, hi) in writes:
            lst = self.mem.setdefault(name, [])
            lst[:] = [e for e in lst if not (lo <= e[0] and e[1] <= hi)]
            lst.append((lo, hi, gidx, True, accum))
        return gidx

    def _cover(self, gidx):
        op = self.ops[gidx]
        e = op["eng"]
        if op["dma"]:
            k = op["qidx"]
            return (("q", e, k % NQROT), 16 * (k // NQROT + 1))
        pos = op["pos"]
        self.incs[e].add(pos)
        op["inc"] = True
        return (("e", e), pos)

    def _all_final(self):
        floor = {}
        for e in ENGS:
            last = None
            for i, o in enumerate(self.ops):
                if o["eng"] == e and not o["dma"]:
                    last = i
            if last is not None:
                key, val = self._cover(last)
                floor[key] = val
            k = self.qcnt[e]
            for r in range(min(k, NQROT)):
                floor[("q", e, r)] = 16 * len(
                    [i for i in range(k) if i % NQROT == r])
        return floor

    def fence(self):
        self.fence_floor = self._all_final()

    def finalize(self):
        return self._all_final()


def emit(nc, plan, final_waits, es):
    import bisect
    incpos = {e: sorted(plan.incs[e]) for e in ENGS}

    def xval(key, val):
        if key[0] == "e":
            return bisect.bisect_right(incpos[key[1]], val)
        return val

    sems = {}
    for e in ENGS:
        sems[("e", e)] = es.enter_context(nc.semaphore(f"S_{e}"))
        for r in range(NQROT):
            sems[("q", e, r)] = es.enter_context(nc.semaphore(f"D_{e}{r}"))
    block = es.enter_context(nc.Block())
    by_eng = {e: [o for o in plan.ops if o["eng"] == e] for e in ENGS}

    def make_section(e):
        ops = by_eng[e]

        def body(eng):
            water = {}
            for o in ops:
                for key, val in sorted(o["waits"].items(), key=lambda x: str(x[0])):
                    val = xval(key, val)
                    if water.get(key, 0) < val:
                        eng.wait_ge(sems[key], val)
                        water[key] = val
                h = o["fn"](eng)
                if o["dma"]:
                    h.then_inc(sems[("q", e, o["qidx"] % NQROT)], 16)
                elif o["inc"]:
                    h.then_inc(sems[("e", e)], 1)
            if e == "sync":
                for key, val in sorted(final_waits.items(), key=lambda x: str(x[0])):
                    val = xval(key, val)
                    if water.get(key, 0) < val:
                        eng.wait_ge(sems[key], val)

        return body

    for e in ENGS:
        getattr(block, e)(make_section(e))


_CONST_CACHE = {}


def host_consts():
    if _CONST_CACHE:
        return _CONST_CACHE
    e = np.arange(1024, dtype=np.int64)
    f = np.arange(513, dtype=np.int64)
    ef = np.outer(e, f)
    mc = ((256 - ef + 512) % 1024) - 512
    ms = ((-ef + 512) % 1024) - 512
    mc[mc == -512] = 0
    ms[ms == -512] = 0
    MF = np.concatenate([mc, ms[:, 1:512]], axis=1).astype(np.int16)
    MA = np.ascontiguousarray(MF.T)
    aw = np.full(1024, 2.0 / 1024.0, dtype=np.float32)
    aw[0] = 1.0 / 1024.0
    aw[512] = 1.0 / 1024.0
    AW = np.ascontiguousarray(aw.reshape(8, 128).T)
    t = np.arange(128)
    U = (t[:, None] <= t[None, :]).astype(np.float16)
    ONESROW = np.ones((1, 128), dtype=np.float16)
    IDENT = np.eye(128, dtype=np.float16)
    _CONST_CACHE.update(MF=MF, MA=MA, AW=AW, U=U, ONESROW=ONESROW, IDENT=IDENT)
    return _CONST_CACHE


def build(nb, seq, upto="B", qlim=None):
    NT = nb * (seq // 128)
    NCH = seq // 128
    rows = nb * seq

    nc = bass.Bass("TRN2", target_bir_lowering=False)
    h_x = nc.dram_tensor("x", [rows, D], F32, kind="ExternalInput").ap()
    h_wq = nc.dram_tensor("wq", [D, D], F32, kind="ExternalInput").ap()
    h_wk = nc.dram_tensor("wk", [D, D], F32, kind="ExternalInput").ap()
    h_wv = nc.dram_tensor("wv", [D, D], F32, kind="ExternalInput").ap()
    h_wg = nc.dram_tensor("wg", [DFF, D], F32, kind="ExternalInput").ap()
    h_wu = nc.dram_tensor("wu", [DFF, D], F32, kind="ExternalInput").ap()
    h_wd = nc.dram_tensor("wd", [D, DFF], F32, kind="ExternalInput").ap()
    h_mf = nc.dram_tensor("mf", [D, D], I16, kind="ExternalInput").ap()
    h_ma = nc.dram_tensor("ma", [D, D], I16, kind="ExternalInput").ap()
    h_aw = nc.dram_tensor("aw", [128, 8], F32, kind="ExternalInput").ap()
    h_u = nc.dram_tensor("u", [128, 128], F16, kind="ExternalInput").ap()
    h_ones = nc.dram_tensor("onesrow", [1, 128], F16, kind="ExternalInput").ap()
    h_id = nc.dram_tensor("ident", [128, 128], F16, kind="ExternalInput").ap()
    h_out = nc.dram_tensor("out", [rows, D], F32, kind="ExternalOutput").ap()
    h_x2 = nc.dram_tensor("x2spill", [rows, D], F32, kind="Internal").ap()
    h_wgq = nc.dram_tensor("wgq", [DFF, D], F16, kind="Internal").ap()
    h_wuq = nc.dram_tensor("wuq", [DFF, D], F16, kind="Internal").ap()
    h_wdq = nc.dram_tensor("wdq", [D, DFF], F16, kind="Internal").ap()

    es = contextlib.ExitStack()
    SB = lambda n, s, d=F32: es.enter_context(nc.sbuf_tensor(n, s, d))

    cpool = SB("cpool", [128, len(ACT_CONSTS)])
    for i, v in enumerate(ACT_CONSTS):
        nc.const_aps.register(F32, v, cpool[:, i:i + 1])
    u16 = SB("u16", [128, 128], F16)
    ones16 = SB("ones16", [1, 128], F16)
    onescol = SB("onescol", [128, 1], F16)
    id16 = SB("id16", [128, 128], F16)
    aw_sb = SB("aw_sb", [128, 8])
    mhalf = SB("mhalf", [128, 1])
    mone = SB("mone", [128, 1])
    x2_sb = [SB(f"x2_{r}", [128, D]) for r in range(2)]

    PSA = [es.enter_context(nc.psum_tensor(f"psa{i}", [128, D], F32))
           for i in range(3)]
    PST = es.enter_context(nc.psum_tensor("pst", [128, D], F16))
    PSC = es.enter_context(nc.psum_tensor("psc", [128, 512], F32))

    p = Plan()

    def rsqrt_pp(src_ap, src_key, tmp_ap, tmp_key, out_ap, out_key, eps):
        p.op("gpsimd", lambda g: g.tensor_scalar(
            tmp_ap, src_ap, eps, None, ALU.add),
            reads=[src_key], writes=[tmp_key])
        p.op("gpsimd", lambda g: g.tensor_tensor(
            out_ap, tmp_ap, mhalf[:], ALU.pow),
            reads=[tmp_key, "mhalf"], writes=[out_key])

    scopeA = es.enter_context(contextlib.ExitStack())
    SA = lambda n, s, d=F32: scopeA.enter_context(nc.sbuf_tensor(n, s, d))
    G16 = {k: SA(f"g16_{k}", [128, 8 * D], F16) for k in "qkv"}
    A16 = SA("a16", [128, 8 * D], F16)
    x_sb = [SA(f"x_{r}", [128, D]) for r in range(2)]
    xln = [SA(f"xln_{r}", [128, D], F16) for r in range(2)]
    xlnT = [SA(f"xlnT_{r}", [128, D], F16) for r in range(2)]
    fqh = [SA(f"fqh_{r}", [128, D], F16) for r in range(2)]
    fk16 = SA("fk16", [128, D], F16)
    fq16 = SA("fq16", [128, D], F16)
    fkh = [SA(f"fkh_{r}", [128, D], F16) for r in range(2)]
    fv_s = [SA(f"fv_{r}", [128, D], F16) for r in range(2)]
    z_s = [SA(f"z_{r}", [128, D], F16) for r in range(2)]
    w_s = [SA(f"w_{r}", [128, D], F16) for r in range(2)]
    wT = [SA(f"wT_{r}", [128, D], F16) for r in range(2)]
    zt = [[SA(f"zt_{r}_{i}", [128, 511], F16) for i in range(3)] for r in range(2)]
    wt = [[SA(f"wt_{r}_{i}", [128, 511], F16) for i in range(3)] for r in range(2)]
    uu2 = [SA(f"uu_{r}", [128, D]) for r in range(2)]
    nn = {k: [SA(f"nn_{k}{r}", [128, 513], F16) for r in range(2)] for k in "qk"}
    stats = [SA(f"stats_{r}", [128, 12]) for r in range(2)]
    mv = [SA(f"mv_{r}", [128, 2]) for r in range(2)]
    vtmp = [SA(f"vtmp_{r}", [128, 1]) for r in range(2)]
    rln = [SA(f"rln_{r}", [128, 1]) for r in range(2)]
    chi = SA("chi", [1, nb * D], F16)
    clo = SA("clo", [1, nb * D], F16)

    p.op("gpsimd", lambda g: [g.memset(cpool[:, i:i + 1], v)
                              for i, v in enumerate(ACT_CONSTS)][-1],
         writes=["cpool"])
    p.op("gpsimd", lambda g: g.memset(mhalf[:], -0.5), writes=["mhalf"])
    p.op("gpsimd", lambda g: g.memset(mone[:], -1.0), writes=["mone"])
    p.op("gpsimd", lambda g: g.memset(onescol[:], 1.0), writes=["onescol"])
    p.op("scalar", lambda s: s.dma_start(u16[:], h_u), writes=["u16"], dma=True)
    p.op("scalar", lambda s: s.dma_start(ones16[:], h_ones), writes=["ones16"], dma=True)
    p.op("scalar", lambda s: s.dma_start(id16[:], h_id), writes=["id16"], dma=True)
    p.op("scalar", lambda s: s.dma_start(aw_sb[:], h_aw), writes=["aw_sb"], dma=True)

    with contextlib.ExitStack() as scopeP:
        SP = lambda n, s, d=F32: scopeP.enter_context(nc.sbuf_tensor(n, s, d))
        F16b = SP("f16b", [128, 8 * D], F16)
        W16 = SP("w16b", [128, 8 * D], F16)
        wraw = [SP("wraw_0", [128, D])] * 2
        angb = SP("angb", [128, D])
        mfs = [SP(f"mfs_{r}", [128, D], I16) for r in range(2)]

        def twiddle_chunk(h_m, ec, dst_fn, dst_key):
            r = ec % 2
            p.op("scalar", lambda s: s.dma_start(
                mfs[r][:], h_m[ec * 128:(ec + 1) * 128, :]),
                writes=[f"mfs_{r}"], dma=True)
            p.op("vector", lambda v: v.tensor_scalar(
                angb[:], mfs[r][:], PI_HI, None, ALU.mult),
                reads=[f"mfs_{r}"], writes=["angb"])
            p.op("vector", lambda v: v.scalar_tensor_tensor(
                angb[:], mfs[r][:], PI_LO, angb[:], ALU.mult, ALU.add),
                reads=[f"mfs_{r}", "angb"], writes=["angb"])
            p.op("scalar", dst_fn, reads=["angb"], writes=[dst_key])

        for ec in range(8):
            twiddle_chunk(h_mf, ec,
                          (lambda s, ec=ec: s.activation(
                              F16b[:, ec * D:(ec + 1) * D], angb[:], ACT.Sin)),
                          ("f16b", ec * D, (ec + 1) * D))

        for kname, h_w in (("q", h_wq), ("k", h_wk), ("v", h_wv)):
            for ec in range(8):
                r = ec % 2
                p.op("scalar", lambda s, ec=ec, h_w=h_w: s.dma_start(
                    wraw[0][:], h_w[ec * 128:(ec + 1) * 128, :]),
                    writes=["wraw_0"], dma=True)
                p.op("vector", lambda v, ec=ec: v.tensor_copy(
                    W16[:, ec * D:(ec + 1) * D], wraw[0][:]),
                    reads=["wraw_0"], writes=[("w16b", ec * D, (ec + 1) * D)])
            for dc in range(8):
                for ec in range(8):
                    for hh in range(2):
                        p.op("tensor", lambda t, dc=dc, ec=ec, hh=hh: t.matmul(
                            PSA[0][:, hh * 512:(hh + 1) * 512],
                            W16[:, ec * D + dc * 128: ec * D + (dc + 1) * 128],
                            F16b[:, ec * D + hh * 512: ec * D + (hh + 1) * 512],
                            start=(ec == 0), stop=(ec == 7)),
                            reads=[("w16b", ec * D + dc * 128, ec * D + (dc + 1) * 128),
                                   ("f16b", ec * D + hh * 512, ec * D + (hh + 1) * 512)],
                            writes=[("psa0", hh * 512, (hh + 1) * 512)], accum=True)
                p.op("vector", lambda v, dc=dc, kname=kname: v.tensor_copy(
                    G16[kname][:, dc * D:(dc + 1) * D], PSA[0][:]),
                    reads=["psa0"], writes=[(f"g16_{kname}", dc * D, (dc + 1) * D)])

        for rc in range(8):
            def mk_a(s, rc=rc):
                return s.activation(angb[:], angb[:], ACT.Sin)
            twiddle_chunk(h_ma, rc, mk_a, "angb")
            p.op("vector", lambda v, rc=rc: v.tensor_scalar(
                A16[:, rc * D:(rc + 1) * D], angb[:], aw_sb[:, rc:rc + 1], None,
                ALU.mult), reads=["angb", "aw_sb"],
                writes=[("a16", rc * D, (rc + 1) * D)])

    def emit_ladder(src_ap3, src_key, w_, nb16, src2_ap, sdeq_ap, sdeq_key,
                    inv6_ap, inv6_key, out_ap3, out_key, bufs, part=None):
        ah, b1, b2, b3, b4 = bufs
        bc = lambda a: a.unsqueeze(2).broadcast_to((128, nb16, 16))
        a3 = lambda tens: tens[:, 0:w_].rearrange("p (b k) -> p b k", k=16)
        a2 = lambda tens: tens[:, 0:w_]
        if part in (None, 1):
            p.op("vector", lambda v: v.tensor_tensor(
                a3(ah), src_ap3, bc(inv6_ap), ALU.mult),
                reads=[src_key, inv6_key], writes=[ah.name])
            p.op("scalar", lambda s: s.activation(
                a2(ah), a2(ah), ACT.Abs),
                reads=[ah.name], writes=[ah.name])
        if part == 1:
            return
        if part != 3:
            p.op("vector", lambda v: v.tensor_scalar(
                a2(b3), a2(ah), 1.25, 0.5, ALU.is_gt, ALU.mult),
                reads=[ah.name], writes=[b3.name])
            p.op("vector", lambda v: v.tensor_scalar(
                a2(b4), a2(ah), 1.75, 0.5, ALU.is_gt, ALU.mult),
                reads=[ah.name], writes=[b4.name])
            p.op("vector", lambda v: v.tensor_tensor(
                a2(b3), a2(b3), a2(b4), ALU.add),
                reads=[b3.name, b4.name], writes=[b3.name])
            p.op("vector", lambda v: v.tensor_scalar(
                a2(b1), a2(ah), 0.25, 0.5, ALU.is_gt, ALU.mult),
                reads=[ah.name], writes=[b1.name])
            p.op("vector", lambda v: v.tensor_scalar(
                a2(b2), a2(ah), 0.75, 0.5, ALU.is_gt, ALU.mult),
                reads=[ah.name], writes=[b2.name])
            p.op("vector", lambda v: v.tensor_tensor(
                a2(b1), a2(b1), a2(b2), ALU.add),
                reads=[b1.name, b2.name], writes=[b1.name])
            p.op("vector", lambda v: v.tensor_scalar(
                a2(b2), a2(ah), 2.5, 1.0, ALU.is_gt, ALU.mult),
                reads=[ah.name], writes=[b2.name])
            p.op("vector", lambda v: v.tensor_scalar(
                a2(b4), a2(ah), 3.5, 1.0, ALU.is_gt, ALU.mult),
                reads=[ah.name], writes=[b4.name])
            p.op("vector", lambda v: v.tensor_tensor(
                a2(b2), a2(b2), a2(b4), ALU.add),
                reads=[b2.name, b4.name], writes=[b2.name])
            p.op("vector", lambda v: v.tensor_scalar(
                a2(b4), a2(ah), 5.0, 2.0, ALU.is_gt, ALU.mult),
                reads=[ah.name], writes=[b4.name])
            p.op("vector", lambda v: v.tensor_tensor(
                a2(b2), a2(b2), a2(b4), ALU.add),
                reads=[b2.name, b4.name], writes=[b2.name])
            p.op("vector", lambda v: v.tensor_tensor(
                a2(b1), a2(b1), a2(b2), ALU.add),
                reads=[b1.name, b2.name], writes=[b1.name])
            p.op("vector", lambda v: v.tensor_tensor(
                a2(b1), a2(b1), a2(b3), ALU.add),
                reads=[b1.name, b3.name], writes=[b1.name])
        if part == 2:
            return
        p.op("scalar", lambda s: s.activation(
            a2(b2), src2_ap, ACT.Sign),
            reads=[src_key], writes=[b2.name])
        p.op("vector", lambda v: v.tensor_tensor(
            a3(b2), a3(b2), bc(sdeq_ap), ALU.mult),
            reads=[b2.name, sdeq_key], writes=[b2.name])
        p.op("vector", lambda v: v.tensor_tensor(
            a2(out_flat) if False else out_ap3, a3(b1), a3(b2), ALU.mult),
            reads=[b1.name, b2.name], writes=[out_key])


    p.fence()
    qraw = [SA(f"qraw_{i}", [128, D]) for i in range(2)]
    qbm = SA("qbm", [128, 64])
    qinv = SA("qinv", [128, 64])
    qsd = SA("qsd", [128, 64])
    q16b = SA("q16b", [128, D], F16)
    lbq = [SA(f"lbq_{i}", [128, D], F16) for i in range(5)]

    wjobs = []
    for h_w, h_q, nrow in ((h_wg, h_wgq, 22), (h_wu, h_wuq, 22)):
        for i in range(nrow):
            wjobs.append((h_w, h_q, i, 0, D))
    for i in range(8):
        for cc_ in range(3):
            off = cc_ * D
            wjobs.append((h_wd, h_wdq, i, off, min(D, DFF - off)))

    def emit_bqjob(wi, part=None):
        h_w, h_q, i, coff, wid = wjobs[wi]
        rq = wi % 2
        nb16 = wid // 16
        qn = h_q.tensor.name
        if part in (2, 3):
            emit_ladder(
                qraw[rq][:, 0:wid].rearrange("p (b k) -> p b k", k=16),
                f"qraw_{rq}", wid, nb16, qraw[rq][:, 0:wid],
                qsd[:, 0:nb16], "qsd", qinv[:, 0:nb16], "qinv",
                q16b[:, 0:wid].rearrange("p (b k) -> p b k", k=16), "q16b", lbq,
                part=part)
            if part == 2:
                return
            p.op("scalar", lambda s, h_q=h_q, i=i, coff=coff, wid=wid:
                 s.dma_start(h_q[i * 128:(i + 1) * 128, coff:coff + wid],
                             q16b[:, 0:wid]),
                 reads=["q16b"], writes=[(qn, i * 100000 + coff,
                                          i * 100000 + coff + wid)], dma=True)
            return
        p.op("scalar", lambda s, h_w=h_w, i=i, coff=coff, wid=wid, rq=rq:
             s.dma_start(qraw[rq][:, 0:wid],
                         h_w[i * 128:(i + 1) * 128, coff:coff + wid]),
             writes=[f"qraw_{rq}"], dma=True)
        p.op("vector", lambda v, rq=rq, wid=wid, nb16=nb16: v.tensor_reduce(
            qbm[:, 0:nb16], qraw[rq][:, 0:wid].rearrange("p (b k) -> p b k", k=16),
            AX.X, ALU.max, apply_absolute_value=True),
            reads=[f"qraw_{rq}"], writes=["qbm"])
        p.op("vector", lambda v, nb16=nb16: v.tensor_scalar(
            qsd[:, 0:nb16], qbm[:, 0:nb16], SIXTH, 1e-12, ALU.mult, ALU.max),
            reads=["qbm"], writes=["qsd"])
        p.op("vector", lambda v, nb16=nb16: v.tensor_scalar(
            qbm[:, 0:nb16], qbm[:, 0:nb16], 1e-30, None, ALU.max),
            reads=["qbm"], writes=["qbm"])
        p.op("vector", lambda v, nb16=nb16: v.reciprocal(
            qinv[:, 0:nb16], qbm[:, 0:nb16]), reads=["qbm"], writes=["qinv"])
        p.op("vector", lambda v, nb16=nb16: v.tensor_scalar(
            qinv[:, 0:nb16], qinv[:, 0:nb16], 6.0, None, ALU.mult),
            reads=["qinv"], writes=["qinv"])
        emit_ladder(
            qraw[rq][:, 0:wid].rearrange("p (b k) -> p b k", k=16), f"qraw_{rq}",
            wid, nb16, qraw[rq][:, 0:wid],
            qsd[:, 0:nb16], "qsd", qinv[:, 0:nb16], "qinv",
            q16b[:, 0:wid].rearrange("p (b k) -> p b k", k=16), "q16b", lbq,
            part=(1 if part == 1 else None))
        if part == 1:
            return
        p.op("scalar", lambda s, h_q=h_q, i=i, coff=coff, wid=wid:
             s.dma_start(h_q[i * 128:(i + 1) * 128, coff:coff + wid],
                         q16b[:, 0:wid]),
             reads=["q16b"], writes=[(qn, i * 100000 + coff,
                                      i * 100000 + coff + wid)], dma=True)

    if upto == "prep":
        final = p.finalize()
        emit(nc, p, final, es)
        es.close()
        return nc

    def tload(t):
        b, c = t % nb, t // nb
        r = t % 2
        row0 = b * seq + c * 128
        p.op("sync", lambda s, row0=row0, r=r: s.dma_start(
            x_sb[r][:], h_x[row0:row0 + 128, :]), writes=[f"x_{r}"], dma=True)

    for t in range(min(2, NT)):
        tload(t)

    def emit_ln_v(t):
        r = t % 2
        xk, xlk, xlTk = f"x_{r}", f"xln_{r}", f"xlnT_{r}"
        p.op("vector", lambda v, r=r: v.bn_stats(stats[r][:, 0:6], x_sb[r][:, 0:512]),
             reads=[xk], writes=[(f"stats_{r}", 0, 6)])
        p.op("vector", lambda v, r=r: v.bn_stats(stats[r][:, 6:12], x_sb[r][:, 512:1024]),
             reads=[xk], writes=[(f"stats_{r}", 6, 12)])
        p.op("vector", lambda v, r=r: v.bn_aggr(mv[r][:], stats[r][:]),
             reads=[f"stats_{r}"], writes=[f"mv_{r}"])
        rsqrt_pp(mv[r][:, 1:2], f"mv_{r}", vtmp[r][:], f"vtmp_{r}",
                 rln[r][:], f"rln_{r}", 1e-5)
        p.op("vector", lambda v, r=r: v.tensor_scalar(
            xln[r][:], x_sb[r][:], mv[r][:, 0:1], rln[r][:], ALU.subtract, ALU.mult),
            reads=[xk, f"mv_{r}", f"rln_{r}"], writes=[xlk])
        p.op("gpsimd", lambda g, r=r: g.tensor_tensor(
            x_sb[r][:], x_sb[r][:], xln[r][:], ALU.add),
            reads=[xk, xlk], writes=[xk])

    def emit_ln_tr(t):
        r = t % 2
        xlk, xlTk = f"xln_{r}", f"xlnT_{r}"
        for j in range(8):
            p.op("tensor", lambda t_, j=j, r=r: t_.transpose(
                PST[:, j * 128:(j + 1) * 128], xln[r][:, j * 128:(j + 1) * 128],
                id16[:]), reads=[(xlk, j * 128, (j + 1) * 128), "id16"],
                writes=[("pst", j * 128, (j + 1) * 128)])
        p.op("vector", lambda v, r=r: v.tensor_copy(xlnT[r][:], PST[:]),
             reads=["pst"], writes=[xlTk])

    def emit_ln(t):
        emit_ln_v(t)
        emit_ln_tr(t)

    def emit_projkv(t):
        r = t % 2
        xlTk = f"xlnT_{r}"
        for kk in range(8):
            for kname, psi in (("k", 0), ("v", 1)):
                for hh in range(2):
                    p.op("tensor", lambda t_, kk=kk, kname=kname, hh=hh, r=r, psi=psi:
                         t_.matmul(PSA[psi][:, hh * 512:(hh + 1) * 512],
                                   xlnT[r][:, kk * 128:(kk + 1) * 128],
                                   G16[kname][:, kk * D + hh * 512: kk * D + (hh + 1) * 512],
                                   start=(kk == 0), stop=(kk == 7)),
                         reads=[(xlTk, kk * 128, (kk + 1) * 128),
                                (f"g16_{kname}", kk * D + hh * 512, kk * D + (hh + 1) * 512)],
                         writes=[(f"psa{psi}", hh * 512, (hh + 1) * 512)], accum=True)

    def emit_projq(t):
        r = t % 2
        xlTk = f"xlnT_{r}"
        for kk in range(8):
            for hh in range(2):
                p.op("tensor", lambda t_, kk=kk, hh=hh, r=r:
                     t_.matmul(PSA[0][:, hh * 512:(hh + 1) * 512],
                               xlnT[r][:, kk * 128:(kk + 1) * 128],
                               G16["q"][:, kk * D + hh * 512: kk * D + (hh + 1) * 512],
                               start=(kk == 0), stop=(kk == 7)),
                     reads=[(xlTk, kk * 128, (kk + 1) * 128),
                            ("g16_q", kk * D + hh * 512, kk * D + (hh + 1) * 512)],
                     writes=[("psa0", hh * 512, (hh + 1) * 512)], accum=True)

    def emit_norm(t, kname, psi, dst):
        r = t % 2
        uk = f"uu_{r}"
        dk = f"{'fqh' if kname == 'q' else 'fkh'}_{r}"
        uux = uu2[r]
        p.op("scalar", lambda s, psi=psi, uux=uux: s.activation(
            uux[:, 0:513], PSA[psi][:, 0:513], ACT.Square),
            reads=[(f"psa{psi}", 0, 513)], writes=[(uk, 0, 513)])
        p.op("scalar", lambda s, psi=psi, uux=uux: s.activation(
            uux[:, 513:1024], PSA[psi][:, 513:1024], ACT.Square),
            reads=[(f"psa{psi}", 513, 1024)], writes=[(uk, 513, 1024)])
        p.op("vector", lambda v, uux=uux: v.tensor_tensor(
            uux[:, 1:512], uux[:, 1:512], uux[:, 513:1024], ALU.add),
            reads=[(uk, 1, 512), (uk, 513, 1024)], writes=[(uk, 1, 512)])
        p.op("scalar", lambda s, uux=uux: s.activation(
            uux[:, 0:513], uux[:, 0:513], ACT.Ln),
            reads=[(uk, 0, 513)], writes=[(uk, 0, 513)])
        p.op("scalar", lambda s, uux=uux: s.activation(
            uux[:, 0:513], uux[:, 0:513], ACT.Exp, scale=0.5),
            reads=[(uk, 0, 513)], writes=[(uk, 0, 513)])
        p.op("scalar", lambda s, uux=uux: s.activation(
            uux[:, 0:513], uux[:, 0:513], ACT.Ln, bias=1e-8, scale=1.0),
            reads=[(uk, 0, 513), "cpool"], writes=[(uk, 0, 513)])
        p.op("scalar", lambda s, kname=kname, r=r, uux=uux: s.activation(
            nn[kname][r][:], uux[:, 0:513], ACT.Exp, scale=-1.0),
            reads=[(uk, 0, 513), "cpool"], writes=[f"nn_{kname}{r}"])
        p.op("vector", lambda v, psi=psi, kname=kname, dst=dst, r=r: v.tensor_tensor(
            dst[r][:, 0:513], PSA[psi][:, 0:513], nn[kname][r][:], ALU.mult),
            reads=[(f"psa{psi}", 0, 513), f"nn_{kname}{r}"], writes=[(dk, 0, 513)])
        p.op("vector", lambda v, psi=psi, kname=kname, dst=dst, r=r: v.tensor_tensor(
            dst[r][:, 513:1024], PSA[psi][:, 513:1024], nn[kname][r][:, 1:512],
            ALU.mult),
            reads=[(f"psa{psi}", 513, 1024), f"nn_{kname}{r}"],
            writes=[(dk, 513, 1024)])

    def emit_norm2(t, kname, srct, srck, dst):
        r = t % 2
        uk = f"uu_{r}"
        dk = f"{'fqh' if kname == 'q' else 'fkh'}_{r}"
        uux = uu2[r]
        p.op("scalar", lambda s, uux=uux: s.activation(
            uux[:, 0:513], srct[:, 0:513], ACT.Square),
            reads=[(srck, 0, 513)], writes=[(uk, 0, 513)])
        p.op("scalar", lambda s, uux=uux: s.activation(
            uux[:, 513:1024], srct[:, 513:1024], ACT.Square),
            reads=[(srck, 513, 1024)], writes=[(uk, 513, 1024)])
        p.op("vector", lambda v, uux=uux: v.tensor_tensor(
            uux[:, 1:512], uux[:, 1:512], uux[:, 513:1024], ALU.add),
            reads=[(uk, 1, 512), (uk, 513, 1024)], writes=[(uk, 1, 512)])
        p.op("scalar", lambda s, uux=uux: s.activation(
            uux[:, 0:513], uux[:, 0:513], ACT.Ln),
            reads=[(uk, 0, 513)], writes=[(uk, 0, 513)])
        p.op("scalar", lambda s, uux=uux: s.activation(
            uux[:, 0:513], uux[:, 0:513], ACT.Exp, scale=0.5),
            reads=[(uk, 0, 513)], writes=[(uk, 0, 513)])
        p.op("scalar", lambda s, uux=uux: s.activation(
            uux[:, 0:513], uux[:, 0:513], ACT.Ln, bias=1e-8, scale=1.0),
            reads=[(uk, 0, 513), "cpool"], writes=[(uk, 0, 513)])
        p.op("scalar", lambda s, kname=kname, r=r, uux=uux: s.activation(
            nn[kname][r][:], uux[:, 0:513], ACT.Exp, scale=-1.0),
            reads=[(uk, 0, 513), "cpool"], writes=[f"nn_{kname}{r}"])
        p.op("vector", lambda v, kname=kname, dst=dst, srct=srct, r=r: v.tensor_tensor(
            dst[r][:, 0:513], srct[:, 0:513], nn[kname][r][:], ALU.mult),
            reads=[(srck, 0, 513), f"nn_{kname}{r}"], writes=[(dk, 0, 513)])
        p.op("vector", lambda v, kname=kname, dst=dst, srct=srct, r=r: v.tensor_tensor(
            dst[r][:, 513:1024], srct[:, 513:1024], nn[kname][r][:, 1:512],
            ALU.mult),
            reads=[(srck, 513, 1024), f"nn_{kname}{r}"],
            writes=[(dk, 513, 1024)])

    def evac_fkfv(tt):
        rr = tt % 2
        p.op("scalar", lambda s: s.copy(fk16[:], PSA[0][:]),
             reads=["psa0"], writes=["fk16"])
        p.op("scalar", lambda s, rr=rr: s.copy(fv_s[rr][:], PSA[1][:]),
             reads=["psa1"], writes=[f"fv_{rr}"])

    emit_ln(0)
    emit_projkv(0)
    evac_fkfv(0)
    for t in range(NT):
        b, c = t % nb, t // nb
        r = t % 2
        row0 = b * seq + c * 128
        xk = f"x_{r}"
        if t + 1 < NT:
            emit_ln_v(t + 1)
        emit_projq(t)
        emit_norm2(t, "k", fk16, "fk16", fkh)
        p.op("scalar", lambda s: s.copy(fq16[:], PSA[0][:]),
             reads=["psa0"], writes=["fq16"])
        if t + 1 < NT:
            emit_ln_tr(t + 1)
        emit_norm2(t, "q", fq16, "fq16", fqh)
        if t + 1 < NT:
            emit_projkv(t + 1)
        fk_, fv_, zk = f"fkh_{r}", f"fv_{r}", f"z_{r}"
        ztr = zt[r]
        p.op("vector", lambda v, r=r: v.tensor_tensor(
            z_s[r][:, 0:513], fkh[r][:, 0:513], fv_s[r][:, 0:513], ALU.mult),
            reads=[(fk_, 0, 513), (fv_, 0, 513)], writes=[(zk, 0, 513)])
        p.op("vector", lambda v, r=r, ztr=ztr: v.tensor_tensor(
            ztr[0][:], fkh[r][:, 513:1024], fv_s[r][:, 513:1024], ALU.mult),
            reads=[(fk_, 513, 1024), (fv_, 513, 1024)], writes=[f"zt_{r}_0"])
        p.op("vector", lambda v, r=r, ztr=ztr: v.tensor_tensor(
            z_s[r][:, 1:512], z_s[r][:, 1:512], ztr[0][:], ALU.subtract),
            reads=[(zk, 1, 512), f"zt_{r}_0"], writes=[(zk, 1, 512)])
        p.op("vector", lambda v, r=r, ztr=ztr: v.tensor_tensor(
            ztr[1][:], fkh[r][:, 1:512], fv_s[r][:, 513:1024], ALU.mult),
            reads=[(fk_, 1, 512), (fv_, 513, 1024)], writes=[f"zt_{r}_1"])
        p.op("vector", lambda v, r=r, ztr=ztr: v.tensor_tensor(
            ztr[2][:], fkh[r][:, 513:1024], fv_s[r][:, 1:512], ALU.mult),
            reads=[(fk_, 513, 1024), (fv_, 1, 512)], writes=[f"zt_{r}_2"])
        p.op("vector", lambda v, r=r, ztr=ztr: v.tensor_tensor(
            z_s[r][:, 513:1024], ztr[1][:], ztr[2][:], ALU.add),
            reads=[f"zt_{r}_1", f"zt_{r}_2"], writes=[(zk, 513, 1024)])
        if t < len(wjobs):
            emit_bqjob(t, part=1)
        bs = b * D
        for hh in range(2):
            p.op("tensor", lambda t_, hh=hh, r=r, c=c: t_.matmul(
                PSA[2][:, hh * 512:(hh + 1) * 512], u16[:],
                z_s[r][:, hh * 512:(hh + 1) * 512],
                start=True, stop=(c == 0)),
                reads=["u16", (zk, hh * 512, (hh + 1) * 512)],
                writes=[("psa2", hh * 512, (hh + 1) * 512)], accum=True)
            if c > 0:
                p.op("tensor", lambda t_, hh=hh, bs=bs: t_.matmul(
                    PSA[2][:, hh * 512:(hh + 1) * 512], ones16[:],
                    chi[0:1, bs + hh * 512: bs + (hh + 1) * 512],
                    start=False, stop=False),
                    reads=["ones16", ("chi", bs + hh * 512, bs + (hh + 1) * 512)],
                    writes=[("psa2", hh * 512, (hh + 1) * 512)], accum=True)
                p.op("tensor", lambda t_, hh=hh, bs=bs: t_.matmul(
                    PSA[2][:, hh * 512:(hh + 1) * 512], ones16[:],
                    clo[0:1, bs + hh * 512: bs + (hh + 1) * 512],
                    start=False, stop=True),
                    reads=["ones16", ("clo", bs + hh * 512, bs + (hh + 1) * 512)],
                    writes=[("psa2", hh * 512, (hh + 1) * 512)], accum=True)
        if c < NCH - 1:
            for hh in range(2):
                p.op("tensor", lambda t_, hh=hh, r=r, c=c: t_.matmul(
                    PSC[0:1, 0:512], onescol[:],
                    z_s[r][:, hh * 512:(hh + 1) * 512],
                    start=True, stop=(c == 0)),
                    reads=["onescol", (zk, hh * 512, (hh + 1) * 512)],
                    writes=["psc"], accum=True)
                if c > 0:
                    p.op("tensor", lambda t_, hh=hh, bs=bs: t_.matmul(
                        PSC[0:1, 0:512], ones16[0:1, 0:1],
                        chi[0:1, bs + hh * 512: bs + (hh + 1) * 512],
                        start=False, stop=False),
                        reads=["ones16", ("chi", bs + hh * 512, bs + (hh + 1) * 512)],
                        writes=["psc"], accum=True)
                    p.op("tensor", lambda t_, hh=hh, bs=bs: t_.matmul(
                        PSC[0:1, 0:512], ones16[0:1, 0:1],
                        clo[0:1, bs + hh * 512: bs + (hh + 1) * 512],
                        start=False, stop=True),
                        reads=["ones16", ("clo", bs + hh * 512, bs + (hh + 1) * 512)],
                        writes=["psc"], accum=True)
                p.op("scalar", lambda s, bs=bs, hh=hh: s.copy(
                    chi[0:1, bs + hh * 512: bs + (hh + 1) * 512], PSC[0:1, 0:512]),
                    reads=["psc"], writes=[("chi", bs + hh * 512, bs + (hh + 1) * 512)])
                p.op("vector", lambda v, bs=bs, hh=hh: v.scalar_tensor_tensor(
                    clo[0:1, bs + hh * 512: bs + (hh + 1) * 512],
                    chi[0:1, bs + hh * 512: bs + (hh + 1) * 512], -1.0,
                    PSC[0:1, 0:512], ALU.mult, ALU.add),
                    reads=[("chi", bs + hh * 512, bs + (hh + 1) * 512), "psc"],
                    writes=[("clo", bs + hh * 512, bs + (hh + 1) * 512)])
        fq_, wk_ = f"fqh_{r}", f"w_{r}"
        wtr = wt[r]
        p.op("vector", lambda v, r=r: v.tensor_tensor(
            w_s[r][:, 0:513], PSA[2][:, 0:513], fqh[r][:, 0:513], ALU.mult),
            reads=[("psa2", 0, 513), (fq_, 0, 513)], writes=[(wk_, 0, 513)])
        p.op("vector", lambda v, r=r, wtr=wtr: v.tensor_tensor(
            wtr[0][:], PSA[2][:, 513:1024], fqh[r][:, 513:1024], ALU.mult),
            reads=[("psa2", 513, 1024), (fq_, 513, 1024)], writes=[f"wt_{r}_0"])
        p.op("vector", lambda v, r=r, wtr=wtr: v.tensor_tensor(
            w_s[r][:, 1:512], w_s[r][:, 1:512], wtr[0][:], ALU.add),
            reads=[(wk_, 1, 512), f"wt_{r}_0"], writes=[(wk_, 1, 512)])
        p.op("vector", lambda v, r=r, wtr=wtr: v.tensor_tensor(
            wtr[1][:], PSA[2][:, 513:1024], fqh[r][:, 1:512], ALU.mult),
            reads=[("psa2", 513, 1024), (fq_, 1, 512)], writes=[f"wt_{r}_1"])
        p.op("vector", lambda v, r=r, wtr=wtr: v.tensor_tensor(
            wtr[2][:], PSA[2][:, 1:512], fqh[r][:, 513:1024], ALU.mult),
            reads=[("psa2", 1, 512), (fq_, 513, 1024)], writes=[f"wt_{r}_2"])
        p.op("vector", lambda v, r=r, wtr=wtr: v.tensor_tensor(
            w_s[r][:, 513:1024], wtr[1][:], wtr[2][:], ALU.subtract),
            reads=[f"wt_{r}_1", f"wt_{r}_2"], writes=[(wk_, 513, 1024)])
        for j in range(8):
            p.op("tensor", lambda t_, j=j, r=r: t_.transpose(
                PST[:, j * 128:(j + 1) * 128], w_s[r][:, j * 128:(j + 1) * 128],
                id16[:]), reads=[(wk_, j * 128, (j + 1) * 128), "id16"],
                writes=[("pst", j * 128, (j + 1) * 128)])
        p.op("scalar", lambda s, r=r: s.copy(wT[r][:], PST[:]),
             reads=["pst"], writes=[f"wT_{r}"])
        for kk in range(8):
            for hh in range(2):
                p.op("tensor", lambda t_, kk=kk, hh=hh, r=r: t_.matmul(
                    PSA[2][:, hh * 512:(hh + 1) * 512],
                    wT[r][:, kk * 128:(kk + 1) * 128],
                    A16[:, kk * D + hh * 512: kk * D + (hh + 1) * 512],
                    start=(kk == 0), stop=(kk == 7)),
                    reads=[(f"wT_{r}", kk * 128, (kk + 1) * 128),
                           ("a16", kk * D + hh * 512, kk * D + (hh + 1) * 512)],
                    writes=[("psa2", hh * 512, (hh + 1) * 512)], accum=True)
        if t < len(wjobs):
            emit_bqjob(t, part=2)
        p.op("vector", lambda v, r=r: v.tensor_tensor(
            x2_sb[r][:], x_sb[r][:], PSA[2][:], ALU.add),
            reads=[xk, "psa2"], writes=[f"x2_{r}"])
        p.op("gpsimd", lambda g, row0=row0, r=r: g.dma_start(
            h_x2[row0:row0 + 128, :], x2_sb[r][:]),
            reads=[f"x2_{r}"], writes=[("x2spill", row0, row0 + 128)], dma=True)
        if t + 2 < NT:
            tload(t + 2)
        if t + 1 < NT:
            evac_fkfv(t + 1)
        if t < len(wjobs):
            emit_bqjob(t, part=3)

    for wi in range(NT, len(wjobs)):
        emit_bqjob(wi, part=1)
        emit_bqjob(wi, part=2)
        emit_bqjob(wi, part=3)

    scopeA.close()
    p.fence()
    if upto == "A":
        final = p.finalize()
        emit(nc, p, final, es)
        es.close()
        return nc

    scopeB = es.enter_context(contextlib.ExitStack())
    SBW = lambda n, s, d=F32: scopeB.enter_context(nc.sbuf_tensor(n, s, d))
    WgT = SBW("wgT", [128, 8 * DFF], F16)
    WuT = SBW("wuT", [128, 8 * DFF], F16)
    WdT = SBW("wdT", [128, 22 * D], F16)
    for kk in range(8):
        p.op("scalar", lambda s, kk=kk: s.dma_start(
            WgT[:, kk * DFF:(kk + 1) * DFF],
            h_wgq[0:DFF, kk * 128:(kk + 1) * 128], transpose=True),
            reads=["wgq", "xpchain"], writes=[("wgT", kk * DFF, (kk + 1) * DFF),
                                              "xpchain"], dma=True)
        p.op("scalar", lambda s, kk=kk: s.dma_start(
            WuT[:, kk * DFF:(kk + 1) * DFF],
            h_wuq[0:DFF, kk * 128:(kk + 1) * 128], transpose=True),
            reads=["wuq", "xpchain"], writes=[("wuT", kk * DFF, (kk + 1) * DFF),
                                              "xpchain"], dma=True)
    for gk in range(22):
        p.op("scalar", lambda s, gk=gk: s.dma_start(
            WdT[:, gk * D:(gk + 1) * D],
            h_wdq[0:D, gk * 128:(gk + 1) * 128], transpose=True),
            reads=["wdq", "xpchain"], writes=[("wdT", gk * D, (gk + 1) * D),
                                              "xpchain"], dma=True)

    p.fence()
    if upto == "Q":
        final = p.finalize()
        emit(nc, p, final, es)
        es.close()
        return nc

    SB2 = lambda n, s, d=F32: scopeB.enter_context(nc.sbuf_tensor(n, s, d))
    xq = [SB2(f"xq_{r}", [128, D], F16) for r in range(2)]
    xqT = [SB2(f"xqT_{r}", [128, D], F16) for r in range(2)]
    hqT = [SB2(f"hqT_{r}", [128, DFF], F16) for r in range(2)]
    d2 = [SB2(f"d2_{r}", [128, D]) for r in range(2)]
    stats2 = [SB2(f"st2_{r}", [128, 12]) for r in range(2)]
    mv2 = [SB2(f"mv2_{r}", [128, 2]) for r in range(2)]
    vtmp2 = [SB2(f"vt2_{r}", [128, 1]) for r in range(2)]
    rln2 = [SB2(f"rln2_{r}", [128, 1]) for r in range(2)]
    bm2 = [SB2(f"bm2_{r}", [128, 64]) for r in range(2)]
    inv2 = [SB2(f"inv2_{r}", [128, 64]) for r in range(2)]
    sdq2 = [SB2(f"sdq2_{r}", [128, 64]) for r in range(2)]
    lbx = [SB2(f"lbx_{i}", [128, D], F16) for i in range(5)]
    silb = [SB2(f"sil_{r}", [128, 512], F16) for r in range(2)]
    upb = [SB2(f"up_{r}", [128, 512], F16) for r in range(2)]
    hbP = [SB2(f"hbP_{m}", [128, 1024], F16) for m in range(2)]
    hqP = [SB2(f"hqP_{m}", [128, 1024], F16) for m in range(2)]
    bmh = SB2("bmh", [128, 64])
    invh = SB2("invh", [128, 64])
    sdh = SB2("sdh", [128, 64])
    lbh = [SB2(f"lbh_{i}", [128, 1024], F16) for i in range(5)]

    JCH = []
    off = 0
    while off < DFF:
        JCH.append((off, min(512, DFF - off)))
        off += 512
    PAIRS = [(0, 1024), (1024, 1024), (2048, 768)]

    def t2load(t):
        b, c = t % nb, t // nb
        r = t % 2
        row0 = b * seq + c * 128
        p.op("sync", lambda s, row0=row0, r=r: s.dma_start(
            x2_sb[r][:], h_x2[row0:row0 + 128, :]),
            reads=[("x2spill", row0, row0 + 128)], writes=[f"x2_{r}"], dma=True)

    for t in range(min(2, NT)):
        t2load(t)

    def emit_xprefix(t, seg="abc"):
        b, c = t % nb, t // nb
        r = t % 2
        x2k = f"x2_{r}"
        if "a" in seg:
            emit_xprefix_a(r, x2k)
        if "b" in seg:
            emit_xprefix_b(r)
        if "c" in seg:
            emit_xprefix_c(r)

    def emit_xprefix_a(r, x2k):
        p.op("vector", lambda v, r=r: v.bn_stats(stats2[r][:, 0:6], x2_sb[r][:, 0:512]),
             reads=[x2k], writes=[(f"st2_{r}", 0, 6)])
        p.op("vector", lambda v, r=r: v.bn_stats(stats2[r][:, 6:12], x2_sb[r][:, 512:1024]),
             reads=[x2k], writes=[(f"st2_{r}", 6, 12)])
        p.op("vector", lambda v, r=r: v.bn_aggr(mv2[r][:], stats2[r][:]),
             reads=[f"st2_{r}"], writes=[f"mv2_{r}"])
        rsqrt_pp(mv2[r][:, 1:2], f"mv2_{r}", vtmp2[r][:], f"vt2_{r}",
                 rln2[r][:], f"rln2_{r}", 1e-5)
        p.op("vector", lambda v, r=r: v.tensor_scalar(
            d2[r][:], x2_sb[r][:], mv2[r][:, 0:1], None, ALU.subtract),
            reads=[x2k, f"mv2_{r}"], writes=[f"d2_{r}"])
        p.op("vector", lambda v, r=r: v.tensor_reduce(
            bm2[r][:], d2[r][:].rearrange("p (b k) -> p b k", k=16), AX.X,
            ALU.max, apply_absolute_value=True),
            reads=[f"d2_{r}"], writes=[f"bm2_{r}"])
        p.op("vector", lambda v, r=r: v.tensor_scalar(
            sdq2[r][:], bm2[r][:], rln2[r][:], SIXTH, ALU.mult, ALU.mult),
            reads=[f"bm2_{r}", f"rln2_{r}"], writes=[f"sdq2_{r}"])
        p.op("vector", lambda v, r=r: v.tensor_scalar(
            sdq2[r][:], sdq2[r][:], 1e-12, None, ALU.max),
            reads=[f"sdq2_{r}"], writes=[f"sdq2_{r}"])
        p.op("vector", lambda v, r=r: v.tensor_scalar(
            bm2[r][:], bm2[r][:], 1e-30, None, ALU.max),
            reads=[f"bm2_{r}"], writes=[f"bm2_{r}"])
        p.op("vector", lambda v, r=r: v.reciprocal(inv2[r][:], bm2[r][:]),
             reads=[f"bm2_{r}"], writes=[f"inv2_{r}"])
        p.op("vector", lambda v, r=r: v.tensor_scalar(
            inv2[r][:], inv2[r][:], 6.0, None, ALU.mult),
            reads=[f"inv2_{r}"], writes=[f"inv2_{r}"])

    def emit_xprefix_b(r):
        emit_ladder(
            d2[r][:].rearrange("p (b k) -> p b k", k=16), f"d2_{r}", D, 64,
            d2[r][:],
            sdq2[r][:], f"sdq2_{r}", inv2[r][:], f"inv2_{r}",
            xq[r][:].rearrange("p (b k) -> p b k", k=16), f"xq_{r}", lbx)

    def emit_xprefix_c(r):
        for j in range(8):
            p.op("tensor", lambda t_, j=j, r=r: t_.matmul(
                PSA[0][:, j * 128:(j + 1) * 128],
                xq[r][:, j * 128:(j + 1) * 128],
                id16[:], start=True, stop=True),
                reads=[(f"xq_{r}", j * 128, (j + 1) * 128), "id16"],
                writes=[("psa0", j * 128, (j + 1) * 128)])
        p.op("scalar", lambda s, r=r: s.copy(xqT[r][:], PSA[0][:]),
             reads=["psa0"], writes=[f"xqT_{r}"])


    emit_xprefix(0)
    pend_tail = None
    for t in range(NT):
        b, c = t % nb, t // nb
        r = t % 2
        rn = (t + 1) % 2
        row0 = b * seq + c * 128
        x2k = f"x2_{r}"
        def emit_gu(j, rr=None):
            joff, wj = JCH[j]
            pj = j % 2
            rr = r if rr is None else rr
            for kk in range(8):
                p.op("tensor", lambda t_, kk=kk, joff=joff, wj=wj, pj=pj, rr=rr:
                     t_.matmul(PSA[pj][:, 0:wj],
                               xqT[rr][:, kk * 128:(kk + 1) * 128],
                               WgT[:, kk * DFF + joff: kk * DFF + joff + wj],
                               start=(kk == 0), stop=(kk == 7)),
                     reads=[(f"xqT_{rr}", kk * 128, (kk + 1) * 128),
                            ("wgT", kk * DFF + joff, kk * DFF + joff + wj)],
                     writes=[(f"psa{pj}", 0, wj)], accum=True)
                p.op("tensor", lambda t_, kk=kk, joff=joff, wj=wj, pj=pj, rr=rr:
                     t_.matmul(PSA[pj][:, 512:512 + wj],
                               xqT[rr][:, kk * 128:(kk + 1) * 128],
                               WuT[:, kk * DFF + joff: kk * DFF + joff + wj],
                               start=(kk == 0), stop=(kk == 7)),
                     reads=[(f"xqT_{rr}", kk * 128, (kk + 1) * 128),
                            ("wuT", kk * DFF + joff, kk * DFF + joff + wj)],
                     writes=[(f"psa{pj}", 512, 512 + wj)], accum=True)

        def emit_h(j):
            joff, wj = JCH[j]
            pj = j % 2
            m2 = (j // 2) % 2
            hoff = (j % 2) * 512
            p.op("scalar", lambda s, pj=pj, wj=wj: s.activation(
                silb[pj][:, 0:wj], PSA[pj][:, 0:wj], ACT.Silu),
                reads=[(f"psa{pj}", 0, wj)], writes=[f"sil_{pj}"])
            p.op("scalar", lambda s, pj=pj, wj=wj: s.copy(
                upb[pj][:, 0:wj], PSA[pj][:, 512:512 + wj]),
                reads=[(f"psa{pj}", 512, 512 + wj)], writes=[f"up_{pj}"])
            p.op("vector", lambda v, pj=pj, wj=wj, m2=m2, hoff=hoff: v.tensor_tensor(
                hbP[m2][:, hoff:hoff + wj], silb[pj][:, 0:wj], upb[pj][:, 0:wj],
                ALU.mult), reads=[f"sil_{pj}", f"up_{pj}"],
                writes=[(f"hbP_{m2}", hoff, hoff + wj)])

        def emit_mid_pair(m):
            poff, pw = PAIRS[m]
            m2 = m % 2
            nbh = pw // 16
            p.op("vector", lambda v, m2=m2, pw=pw, nbh=nbh: v.tensor_reduce(
                bmh[:, 0:nbh], hbP[m2][:, 0:pw].rearrange("p (b k) -> p b k", k=16),
                AX.X, ALU.max, apply_absolute_value=True),
                reads=[f"hbP_{m2}"], writes=["bmh"])
            p.op("vector", lambda v, nbh=nbh: v.tensor_scalar(
                sdh[:, 0:nbh], bmh[:, 0:nbh], SIXTH, 1e-12, ALU.mult, ALU.max),
                reads=["bmh"], writes=["sdh"])
            p.op("vector", lambda v, nbh=nbh: v.tensor_scalar(
                bmh[:, 0:nbh], bmh[:, 0:nbh], 1e-30, None, ALU.max),
                reads=["bmh"], writes=["bmh"])
            p.op("vector", lambda v, nbh=nbh: v.reciprocal(
                invh[:, 0:nbh], bmh[:, 0:nbh]), reads=["bmh"], writes=["invh"])
            p.op("vector", lambda v, nbh=nbh: v.tensor_scalar(
                invh[:, 0:nbh], invh[:, 0:nbh], 6.0, None, ALU.mult),
                reads=["invh"], writes=["invh"])
            emit_ladder(
                hbP[m2][:, 0:pw].rearrange("p (b k) -> p b k", k=16), f"hbP_{m2}",
                pw, nbh, hbP[m2][:, 0:pw],
                sdh[:, 0:nbh], "sdh", invh[:, 0:nbh], "invh",
                hqP[m2][:, 0:pw].rearrange("p (b k) -> p b k", k=16), f"hqP_{m2}",
                lbh)

        def emit_trdown_pair(m):
            poff, pw = PAIRS[m]
            m2 = m % 2
            for kk in range(pw // 128):
                p.op("tensor", lambda t_, kk=kk, m2=m2: t_.transpose(
                    PST[:, kk * 128:(kk + 1) * 128],
                    hqP[m2][:, kk * 128:(kk + 1) * 128],
                    id16[:]), reads=[(f"hqP_{m2}", kk * 128, (kk + 1) * 128), "id16"],
                    writes=[("pst", kk * 128, (kk + 1) * 128)])
            p.op("scalar", lambda s, poff=poff, pw=pw, r=r: s.copy(
                hqT[r][:, poff:poff + pw], PST[:, 0:pw]),
                reads=[("pst", 0, pw)], writes=[(f"hqT_{r}", poff, poff + pw)])
            for kk in range(pw // 128):
                gk = poff // 128 + kk
                for hh in range(2):
                    p.op("tensor", lambda t_, gk=gk, hh=hh, r=r:
                         t_.matmul(PSA[2][:, hh * 512:(hh + 1) * 512],
                                   hqT[r][:, gk * 128:(gk + 1) * 128],
                                   WdT[:, gk * D + hh * 512: gk * D + (hh + 1) * 512],
                                   start=(gk == 0), stop=(gk == 21)),
                         reads=[(f"hqT_{r}", gk * 128, (gk + 1) * 128),
                                ("wdT", gk * D + hh * 512, gk * D + (hh + 1) * 512)],
                         writes=[(f"psa2", hh * 512, (hh + 1) * 512)], accum=True)

        if t == 0:
            emit_gu(0)
            emit_gu(1)
            emit_h(0)
            emit_h(1)
        emit_mid_pair(0)
        if pend_tail is not None:
            pend_tail()
            pend_tail = None
        for m in range(3):
            if 2 * m + 2 < 6:
                emit_gu(2 * m + 2)
                emit_h(2 * m + 2)
                emit_gu(2 * m + 3)
                emit_h(2 * m + 3)
            if m == 2 and t + 1 < NT:
                emit_gu(0, rn)
                emit_h(0)
                emit_gu(1, rn)
                emit_h(1)
            emit_trdown_pair(m)
            if t + 1 < NT and m == 1:
                emit_xprefix_c(rn)
            if 2 * m + 2 < 6:
                emit_mid_pair(m + 1)
            if t + 1 < NT and m == 0:
                emit_xprefix_a(rn, f"x2_{rn}")
                emit_xprefix_b(rn)

        def tail(r=r, row0=row0, t=t, x2k=x2k):
            p.op("vector", lambda v, r=r: v.tensor_tensor(
                x2_sb[r][:], x2_sb[r][:], PSA[2][:], ALU.add),
                reads=[x2k, "psa2"], writes=[x2k])
            p.op("gpsimd", lambda g, row0=row0, r=r: g.dma_start(
                h_out[row0:row0 + 128, :], x2_sb[r][:]),
                reads=[f"x2_{r}"], writes=[("outhbm", row0, row0 + 128)], dma=True)
            if t + 2 < NT:
                t2load(t + 2)
        if t + 1 < NT:
            pend_tail = tail
        else:
            tail()

    final = p.finalize()
    emit(nc, p, final, es)
    es.close()
    return nc


def kernel(L0_attn_x, L0_attn_Wq, L0_attn_Wk, L0_attn_Wv,
           L0_ffn_w_gate, L0_ffn_w_up, L0_ffn_w_down):
    x = np.ascontiguousarray(np.asarray(L0_attn_x, dtype=np.float32))
    B, S, _ = x.shape
    nb = B // NCORES
    cc = host_consts()
    nc = build(nb, S)
    com = {
        "wq": np.ascontiguousarray(np.asarray(L0_attn_Wq, dtype=np.float32)),
        "wk": np.ascontiguousarray(np.asarray(L0_attn_Wk, dtype=np.float32)),
        "wv": np.ascontiguousarray(np.asarray(L0_attn_Wv, dtype=np.float32)),
        "wg": np.ascontiguousarray(np.asarray(L0_ffn_w_gate, dtype=np.float32)),
        "wu": np.ascontiguousarray(np.asarray(L0_ffn_w_up, dtype=np.float32)),
        "wd": np.ascontiguousarray(np.asarray(L0_ffn_w_down, dtype=np.float32)),
        "mf": cc["MF"], "ma": cc["MA"], "aw": cc["AW"], "u": cc["U"],
        "onesrow": cc["ONESROW"], "ident": cc["IDENT"],
    }
    in_maps = []
    for c in range(NCORES):
        m = dict(com)
        m["x"] = np.ascontiguousarray(x[c * nb:(c + 1) * nb].reshape(nb * S, D))
        in_maps.append(m)
    res = run_bass_kernel_spmd(nc, in_maps, core_ids=list(range(NCORES)))
    out = np.concatenate(
        [np.asarray(r["out"], dtype=np.float32).reshape(nb, S, D)
         for r in res.results], axis=0)
    return out
